# version 30
# baseline (speedup 1.0000x reference)
import numpy as np
import ml_dtypes
from contextlib import ExitStack

import concourse.bass as bass
import concourse.mybir as mybir
from concourse.bass_utils import run_bass_kernel_spmd

F32 = mybir.dt.float32
BF16 = mybir.dt.bfloat16
AF = mybir.ActivationFunctionType
ALU = mybir.AluOpType
AX = mybir.AxisListType

B, L, D = 2, 4096, 2048
HKV, HD = 8, 64
SCALE = HD ** (-0.5)
CAP = 512
E = 8
FF = 8192
TPC = 1024
NI = 8
NDC = D // 128
NQB = 16
NPAIR = 4
NFB = FF // 128
NGB = (B * L) // 128
EPS = 1e-5
KV_ROWS = 4 * 128


def pos_of(i, j):
    return 4 * i + (i + j) % 4


def owner_of(p):
    i = p // 4
    j = (p % 4 - i) % 4
    return j, i


ENGINES = ("sync", "gpsimd", "tensor", "vector", "scalar")
DMA_SEM_NAMES = (
    "s_const", "s_x0", "s_x1", "s_x2", "s_x3", "s_kstg", "s_vstg",
    "s_ak0", "s_ak1", "s_av0", "s_av1", "s_lg", "s_rt0", "s_rt1",
    "s_strip", "s_sp0", "s_sp1", "s_wt", "s_out0", "s_out1", "s_swp",
    "s_wl", "s_wl2", "s_wl3",
    "g_const", "g_tiny", "g_wc0", "g_wc1", "g_wv", "g_wo0", "g_wo1",
    "g_w10", "g_w11", "g_w2")


class Prog:
    def __init__(self):
        self.ops = []
        self.ctr = {}

    def op(self, eng, fn, waits=(), inc=None):
        ws = [(w[0], w[1]) for w in waits if w is not None]
        if fn is not None and eng == "vector" and self.ctr.get("vp", 0) > 0:
            ws.append(("vp", self.ctr["vp"]))
        if fn is not None and eng == "scalar" and self.ctr.get("ap", 0) > 0:
            ws.append(("ap", self.ctr["ap"]))
        lbl = None
        if inc is not None:
            amt = 16 if (inc.startswith("s_") or inc.startswith("g_")) else 1
            self.ctr[inc] = self.ctr.get(inc, 0) + amt
            lbl = (inc, self.ctr[inc], amt)
        self.ops.append((eng, fn, ws, lbl))
        return (lbl[0], lbl[1]) if lbl else None

    def cur(self, sem):
        return (sem, self.ctr[sem]) if self.ctr.get(sem, 0) > 0 else None

    def barrier(self, skip=()):
        lbls = []
        for eng in ("gpsimd", "tensor", "vector", "scalar"):
            lbls.append(self.op(eng, lambda e: e.drain(), inc="b" + eng[0]))
        for s in DMA_SEM_NAMES + ("ccs",):
            if s in skip:
                continue
            c = self.cur(s)
            if c:
                lbls.append(c)
        for eng in ENGINES:
            self.op(eng, None, waits=lbls)
        return lbls

    def emit(self, block, sems):
        for eng_name in ENGINES:
            def section(e, _eng=eng_name):
                seen = {}
                for (en, fn, waits, lbl) in self.ops:
                    if en != _eng:
                        continue
                    byname = {}
                    for (s, v) in waits:
                        byname[s] = max(byname.get(s, 0), v)
                    for s, v in byname.items():
                        if v > seen.get(s, 0):
                            e.wait_ge(sems[s], v)
                            seen[s] = v
                    if fn is None:
                        continue
                    inst = fn(e)
                    if lbl is not None:
                        inst.then_inc(sems[lbl[0]], lbl[2])
            getattr(block, eng_name)(section)


def build_nc():
    nc = bass.Bass("TRN2", target_bir_lowering=False)
    P = Prog()

    def din(name, shape, dt=F32):
        return nc.dram_tensor(name, shape, dt, kind="ExternalInput").ap()

    xT_in = din("xT_in", [D, TPC])
    wq_t = din("wq_t", [NQB * 128, D])
    wk_t = din("wk_t", [NPAIR * 128, D])
    wv_t = din("wv_t", [D, HKV * HD])
    wo_t = din("wo_t", [NDC * 128, D])
    rt_t = din("rt_t", [128, NDC * E])
    w1_t = din("w1_t", [NFB * 128, D])
    b1_t = din("b1_t", [128, NFB])
    w2_t = din("w2_t", [FF, D])
    b2_t = din("b2_t", [128, NDC])
    cq2 = din("cq2", [128, TPC])
    sq2 = din("sq2", [128, TPC])
    ck2 = din("ck2", [128, TPC])
    sk2 = din("sk2", [128, TPC])
    mask_in = din("mask_in", [128, NI * 4 * 128])
    ut128 = din("ut128", [128, 128])
    ut64s = din("ut64s", [64, 64])
    eye_f = din("eye_f", [128, 128])
    ones_r = din("ones_r", [1, 128])
    onescol = din("onescol", [128, 1])
    ownsel = din("ownsel", [64, 8])

    xoutT = nc.dram_tensor("xoutT", [D, TPC], F32, kind="ExternalOutput").ap()

    kp_ag_in = [nc.dram_tensor(f"kp_ag_in{p}", [128, TPC], BF16) for p in range(4)]
    kp_ag_out = [nc.dram_tensor(f"kp_ag_out{p}", [512, TPC], BF16) for p in range(4)]
    v_ag_in = nc.dram_tensor("v_ag_in", [TPC, 8 * 65], BF16)
    v_ag_out = nc.dram_tensor("v_ag_out", [4 * TPC, 8 * 65], BF16)
    lg_ag_in = nc.dram_tensor("lg_ag_in", [TPC, E], F32)
    lg_ag_out = nc.dram_tensor("lg_ag_out", [8 * TPC, E], F32)
    scr_dram = nc.dram_tensor("scr_dram", [8, 1024], F32)
    x2spill = nc.dram_tensor("x2spill", [D, TPC], F32)

    root = ExitStack()

    def SB(stack, name, shape, dt=F32):
        return stack.enter_context(nc.sbuf_tensor(name, shape, dt))

    def PS(stack, name, shape, dt=F32):
        return stack.enter_context(nc.psum_tensor(name, shape, dt))

    with root:
        xlnT = SB(root, "xlnT", [128, NDC * TPC], BF16)
        ut128_s = SB(root, "ut128_s", [128, 128])
        ut64s_s = SB(root, "ut64s_s", [64, 64])
        eye_s = SB(root, "eye_s", [128, 128])
        ones_s = SB(root, "ones_s", [1, 128])
        onescol_s = SB(root, "onescol_s", [128, 1])
        ones_b = SB(root, "ones_b", [1, 128], BF16)
        onescol_b = SB(root, "onescol_b", [128, 1], BF16)
        ownsel_s = SB(root, "ownsel_s", [64, 8])
        b1_s = SB(root, "b1_s", [128, NFB])
        b2_s = SB(root, "b2_s", [128, NDC])
        rt_b = SB(root, "rt_b", [128, NDC * E], BF16)
        rowsc = SB(root, "rowsc", [1, 2048])
        mbc = SB(root, "mbc", [128, 1024])
        rbc = SB(root, "rbc", [128, 1024])
        xsqb = SB(root, "xsqb", [128, 1024], BF16)
        xtile = SB(root, "xtile", [128, 4 * 1024])
        xbtile = SB(root, "xbtile", [128, 1024], BF16)
        wtbc = SB(root, "wtbc", [128, 1024])
        wt_s = SB(root, "wt_s", [128, 8])
        wtp = SB(root, "wtp", [128, 16])
        ohall = SB(root, "ohall", [128, NGB * 16])
        att = ExitStack()
        qT = SB(att, "qT", [128, NQB * TPC], BF16)
        otile = SB(att, "otile", [128, NQB * TPC], BF16)
        mask_b = SB(att, "mask_b", [128, NI * 4 * 128], BF16)
        ropestk = ExitStack()
        cq_s = SB(ropestk, "cq_s", [128, TPC], BF16)
        sq_s = SB(ropestk, "sq_s", [128, TPC], BF16)
        ck_s = SB(ropestk, "ck_s", [128, TPC], BF16)
        sk_s = SB(ropestk, "sk_s", [128, TPC], BF16)

        sems = {}
        for s in DMA_SEM_NAMES + ("ccs", "tp", "vp", "ap", "gp", "pp",
                                  "bg", "bt", "bv", "bs"):
            sems[s] = root.enter_context(nc.semaphore(s))

        block = root.enter_context(nc.Block())

        cl = None
        for dst, src in ((ut128_s, ut128), (ut64s_s, ut64s), (eye_s, eye_f),
                         (ones_s, ones_r), (onescol_s, onescol), (b1_s, b1_t),
                         (b2_s, b2_t), (ownsel_s, ownsel)):
            cl = P.op("sync", lambda e, d=dst, s=src: e.dma_start(out=d[:], in_=s),
                      inc="s_const")
        cg = None
        for dst, src in ((ones_b, ones_r), (onescol_b, onescol)):
            cg = P.op("gpsimd", lambda e, d=dst, s=src: e.dma_start(out=d[:], in_=s),
                      inc="g_tiny")
        CONST_TINY = cg
        for dst, src in ((ck_s, ck2), (sk_s, sk2), (cq_s, cq2), (sq_s, sq2),
                         (mask_b, mask_in)):
            cg = P.op("gpsimd", lambda e, d=dst, s=src: e.dma_start(out=d[:], in_=s),
                      inc="g_const")
        cg = P.op("gpsimd", lambda e: e.dma_start(
            out=rt_b[:], in_=rt_t), inc="g_const")
        CONST_L, CONST_G = cl, cg

        def ln_rows_and_apply(stack, tag, pmean, psq, stats_done, xsrc, extra_ld_dep,
                              sb_src=None):
            lst = ExitStack()
            pb = PS(lst, "pb" + tag, [128, 1024])
            pb2 = PS(lst, "pb2" + tag, [128, 1024])
            r1 = P.op("vector", lambda e: e.tensor_scalar(
                out=rowsc[0:1, 0:1024], in0=pmean[0:1, :], scalar1=-1.0 / D,
                scalar2=None, op0=ALU.mult), waits=[stats_done], inc="vp")
            r2 = P.op("vector", lambda e: e.tensor_scalar(
                out=rowsc[0:1, 1024:2048], in0=psq[0:1, :], scalar1=1.0 / D,
                scalar2=EPS, op0=ALU.mult, op1=ALU.add), waits=[stats_done], inc="vp")
            r3 = P.op("vector", lambda e: e.tensor_tensor(
                out=mbc[0:1, 0:1024], in0=rowsc[0:1, 0:1024],
                in1=rowsc[0:1, 0:1024], op=ALU.mult), waits=[r1], inc="vp")
            r4 = P.op("vector", lambda e: e.tensor_tensor(
                out=rowsc[0:1, 1024:2048], in0=rowsc[0:1, 1024:2048],
                in1=mbc[0:1, 0:1024], op=ALU.subtract), waits=[r2, r3], inc="vp")
            r5 = P.op("scalar", lambda e: e.activation(
                out=mbc[0:1, 0:1024], in_=rowsc[0:1, 1024:2048], func=AF.Ln),
                waits=[r4], inc="ap")
            r6 = P.op("scalar", lambda e: e.activation(
                out=rowsc[0:1, 1024:2048], in_=mbc[0:1, 0:1024], func=AF.Exp,
                scale=-0.5), waits=[r5], inc="ap")
            bm = None
            for h in range(2):
                P.op("tensor", lambda e, h=h: e.matmul(
                    pb[:, h * 512:(h + 1) * 512], ones_s[:],
                    rowsc[0:1, h * 512:(h + 1) * 512], start=True, stop=True),
                    waits=[r1, r6, CONST_L])
                bm = P.op("tensor", lambda e, h=h: e.matmul(
                    pb2[:, h * 512:(h + 1) * 512], ones_s[:],
                    rowsc[0:1, 1024 + h * 512:1536 + h * 512], start=True, stop=True),
                    waits=[r6], inc="tp" if h == 1 else None)
            e1 = P.op("scalar", lambda e: e.copy(out=mbc[:], in_=pb[:]),
                      waits=[bm], inc="ap")
            e2 = P.op("scalar", lambda e: e.copy(out=rbc[:], in_=pb2[:]),
                      waits=[bm], inc="ap")
            lst.close()
            out_done = [None] * NDC
            prev = [None, None, None, None]
            for dc in range(NDC):
                sl = dc % 4
                xt = xtile[:, sl * 1024:(sl + 1) * 1024]
                if sb_src is None:
                    ld = P.op("sync", lambda e, d=xt, dc=dc: e.dma_start(
                        out=d, in_=xsrc[dc * 128:(dc + 1) * 128, :]),
                        waits=[prev[sl], e1] + (extra_ld_dep or []),
                        inc=f"s_x{sl}")
                    a1 = P.op("vector", lambda e, s=xt: e.tensor_tensor(
                        out=s, in0=s, in1=mbc[:], op=ALU.add),
                        waits=[ld, e2], inc="vp")
                else:
                    src, w0 = sb_src(dc)
                    a1 = P.op("vector", lambda e, s=xt, src=src: e.tensor_tensor(
                        out=s, in0=src, in1=mbc[:], op=ALU.add),
                        waits=[prev[sl], e1, e2] + w0, inc="vp")
                a2 = P.op("vector", lambda e, s=xt, dc=dc: e.tensor_tensor(
                    out=xlnT[:, dc * TPC:(dc + 1) * TPC], in0=s, in1=rbc[:],
                    op=ALU.mult), waits=[a1], inc="vp")
                prev[sl] = a2
                out_done[dc] = a2
            return out_done

        with ExitStack() as ph:
            pmean = PS(ph, "pmean1", [1, 1024])
            psq = PS(ph, "psq1", [1, 1024])
            prev = [None, None, None, None]
            last = None
            cst_lab = [None] * NDC
            for dc in range(NDC):
                sl = dc % 4
                xt = xtile[:, sl * 1024:(sl + 1) * 1024]
                ob = otile[:, dc * TPC:(dc + 1) * TPC]
                ld = P.op("sync", lambda e, d=xt, dc=dc: e.dma_start(
                    out=d, in_=xT_in[dc * 128:(dc + 1) * 128, :]),
                    waits=[prev[sl]], inc=f"s_x{sl}")
                cst = P.op("scalar", lambda e, s=xt, ob=ob: e.copy(
                    out=ob, in_=s), waits=[ld], inc="ap")
                cst_lab[dc] = cst
                sqr = P.op("scalar", lambda e, ob=ob: e.activation(
                    out=xsqb[:], in_=ob, func=AF.Square),
                    waits=[cst, last], inc="ap")
                for h in range(2):
                    P.op("tensor", lambda e, dc=dc, h=h, ob=ob: e.matmul(
                        pmean[0:1, h * 512:(h + 1) * 512], onescol_b[:],
                        ob[:, h * 512:(h + 1) * 512],
                        start=(dc == 0), stop=(dc == NDC - 1)),
                        waits=[cst, CONST_TINY])
                    last = P.op("tensor", lambda e, dc=dc, h=h: e.matmul(
                        psq[0:1, h * 512:(h + 1) * 512], onescol_b[:],
                        xsqb[:, h * 512:(h + 1) * 512],
                        start=(dc == 0), stop=(dc == NDC - 1)),
                        waits=[sqr], inc="tp" if h == 1 else None)
                prev[sl] = cst
            ln1_done = ln_rows_and_apply(
                ph, "L1", pmean, psq, last, None, None,
                sb_src=lambda dc: (otile[:, dc * TPC:(dc + 1) * TPC],
                                   [cst_lab[dc], cst_lab[12 + dc % 4]]))
        P.barrier()

        with ExitStack() as ph:
            wcol = [SB(ph, "wcolA", [128, NDC * 128], BF16),
                    SB(ph, "wcolB", [128, NDC * 128], BF16)]
            wv_sb = SB(ph, "wv_sb", [128, NDC * 512], BF16)
            kloc = SB(ph, "kloc", [128, NPAIR * TPC], BF16)
            vext = SB(ph, "vext", [128, 8 * 65], BF16)
            ropes = SB(ph, "ropes", [128, TPC], BF16)
            pqk = [PS(ph, "pqkA", [128, TPC]), PS(ph, "pqkB", [128, TPC])]
            pv_ = PS(ph, "pv", [128, 512])

            wv_ld = P.op("gpsimd", lambda e: e.dma_start(
                out=wv_sb[:].rearrange("p (c n) -> p c n", n=512),
                in_=wv_t.rearrange("(c p) n -> p c n", p=128)), inc="g_wv")

            def wcol_load(t, src, col, war):
                return [P.op("gpsimd", lambda e, b=wcol[t % 2], s=src, c=col: e.dma_start(
                    out=b[:], in_=s[c * 128:(c + 1) * 128, :]),
                    waits=[war], inc="g_wc0" if t % 2 == 0 else "g_wc1")]

            rope_prev = [None]

            def rope(base_lbl, tile, ctab, stab):
                sw = None
                for q4 in range(4):
                    dst = ropes[q4 * 32:(q4 + 1) * 32, :]
                    src = tile[(q4 ^ 1) * 32:((q4 ^ 1) + 1) * 32, :]
                    sw = P.op("sync", lambda e, d=dst, s=src: e.dma_start(
                        out=d, in_=s), waits=[base_lbl, rope_prev[0]], inc="s_swp")
                o1 = P.op("vector", lambda e, t=tile, st=stab: e.tensor_tensor(
                    out=ropes[:], in0=ropes[:], in1=st[:], op=ALU.mult),
                    waits=[sw, CONST_G], inc="vp")
                o2 = P.op("vector", lambda e, t=tile, c=ctab: e.tensor_tensor(
                    out=t, in0=t, in1=c[:], op=ALU.mult), waits=[sw], inc="vp")
                done = P.op("vector", lambda e, t=tile: e.tensor_tensor(
                    out=t, in0=t, in1=ropes[:], op=ALU.add), waits=[o1, o2], inc="vp")
                rope_prev[0] = done
                return done

            kv_groups = [[0, 1, 2, 3], [4, 5, 6, 7]]
            ag_k_p = [None] * NPAIR
            kmm = [None] * NPAIR
            kevac = [None] * NPAIR
            krope = [None] * NPAIR
            ag_pend = []
            for t in range(NPAIR):
                kw = wcol_load(t, wk_t, t, war=kmm[t - 2] if t >= 2 else None)
                if t >= 2 and ag_pend:
                    ag_pend.pop(0)()
                pp = pqk[t % 2]
                for dc in range(NDC):
                    for h in range(2):
                        kmm[t] = P.op("tensor", lambda e, b=wcol[t % 2], p=pp, dc=dc, h=h: e.matmul(
                            p[:, h * 512:(h + 1) * 512], b[:, dc * 128:(dc + 1) * 128],
                            xlnT[:, dc * TPC + h * 512: dc * TPC + (h + 1) * 512],
                            start=(dc == 0), stop=(dc == NDC - 1)),
                            waits=kw + [ln1_done[dc] if t == 0 else None,
                                        kevac[t - 2] if t >= 2 else None],
                            inc="tp" if (dc == NDC - 1 and h == 1) else None)
                kt = kloc[:, t * TPC:(t + 1) * TPC]
                kevac[t] = P.op("scalar", lambda e, p=pp, d=kt: e.copy(out=d, in_=p[:]),
                                waits=[kmm[t]], inc="ap")
                krope[t] = rope(kevac[t], kt, ck_s, sk_s)
                kst = P.op("sync", lambda e, t=t, d=kt: e.dma_start(
                    out=kp_ag_in[t].ap(), in_=d),
                    waits=[krope[t]], inc="s_kstg")

                def mk_ag(t=t, kst=kst):
                    def go():
                        ag_k_p[t] = P.op("gpsimd", lambda e, t=t: e.collective_compute(
                            "AllGather", ALU.bypass, replica_groups=kv_groups,
                            ins=[kp_ag_in[t].ap().opt()],
                            outs=[kp_ag_out[t].ap().opt()]),
                            waits=[kst], inc="ccs")
                    return go
                ag_pend.append(mk_ag())
            qw01 = [wcol_load(t_, wq_t, t_, war=kmm[NPAIR - 1]) for t_ in range(2)]
            for go in ag_pend:
                go()
            ag_pend = []

            vstage = [None] * NI
            ones_set = None
            for kv in range(8):
                ones_set = P.op("vector", lambda e, kv=kv: e.memset(
                    vext[:, kv * 65 + 64: kv * 65 + 65], 1.0), inc="vp")
            for tb in range(NI):
                vmm = None
                for dc in range(NDC):
                    vmm = P.op("tensor", lambda e, dc=dc, tb=tb: e.matmul(
                        pv_[:, :], xlnT[:, dc * TPC + tb * 128: dc * TPC + (tb + 1) * 128],
                        wv_sb[:, dc * 512:(dc + 1) * 512],
                        start=(dc == 0), stop=(dc == NDC - 1)),
                        waits=[wv_ld, vstage[tb - 1] if tb >= 1 else None],
                        inc="tp" if dc == NDC - 1 else None)
                ev = None
                for kv in range(8):
                    ev = P.op("scalar", lambda e, kv=kv: e.copy(
                        out=vext[:, kv * 65: kv * 65 + 64],
                        in_=pv_[:, kv * 64:(kv + 1) * 64]), waits=[vmm], inc="ap")
                vstage[tb] = P.op("sync", lambda e, tb=tb: e.dma_start(
                    out=v_ag_in.ap()[tb * 128:(tb + 1) * 128, :], in_=vext[:]),
                    waits=[ev, ones_set], inc="s_vstg")

            ag_v = P.op("gpsimd", lambda e: e.collective_compute(
                "AllGather", ALU.bypass, replica_groups=kv_groups,
                ins=[v_ag_in.ap().opt()], outs=[v_ag_out.ap().opt()]),
                waits=[P.cur("s_vstg")], inc="ccs")

            qmm = [None] * NQB
            qevac = [None] * NQB
            qrope = [None] * NQB
            for t in range(NQB):
                qw = (qw01[t] if t < 2
                      else wcol_load(t, wq_t, t, war=qmm[t - 2]))
                pp = pqk[t % 2]
                for dc in range(NDC):
                    for h in range(2):
                        qmm[t] = P.op("tensor", lambda e, b=wcol[t % 2], p=pp, dc=dc, h=h: e.matmul(
                            p[:, h * 512:(h + 1) * 512], b[:, dc * 128:(dc + 1) * 128],
                            xlnT[:, dc * TPC + h * 512: dc * TPC + (h + 1) * 512],
                            start=(dc == 0), stop=(dc == NDC - 1)),
                            waits=qw + [qevac[t - 2] if t >= 2 else kevac[NPAIR - 1]],
                            inc="tp" if (dc == NDC - 1 and h == 1) else None)
                qt = qT[:, t * TPC:(t + 1) * TPC]
                qevac[t] = P.op("scalar", lambda e, p=pp, d=qt: e.copy(out=d, in_=p[:]),
                                waits=[qmm[t]], inc="ap")
                qrope[t] = rope(qevac[t], qt, cq_s, sq_s)
        P.barrier()
        ropestk.close()

        with ExitStack() as ph:
            kpair = [SB(ph, "kpairA", [128, 4096], BF16),
                     SB(ph, "kpairB", [128, 4096], BF16)]
            vpair = [SB(ph, "vpairA", [128, 32 * 130], BF16),
                     SB(ph, "vpairB", [128, 32 * 130], BF16)]
            ptile = [SB(ph, "ptileA", [128, 2048], BF16),
                     SB(ph, "ptileB", [128, 2048], BF16)]
            osb = [SB(ph, "osbA", [64, 1024], BF16), SB(ph, "osbB", [64, 1024], BF16)]
            dnb = [SB(ph, "dnbA", [64, 1024], BF16), SB(ph, "dnbB", [64, 1024], BF16)]
            rdn = SB(ph, "rdn", [64, 2048], BF16)
            pS = [PS(ph, "pSA", [128, 1024]), PS(ph, "pSB", [128, 1024])]
            pOq = [[PS(ph, f"pO{par}{h}", [128, 512]) for h in range(2)]
                   for par in range(2)]
            rb_init = P.op("vector", lambda e: e.memset(dnb[0][:], 1.0), inc="vp")
            rb_init = P.op("vector", lambda e: e.memset(dnb[1][:], 1.0),
                           waits=[rb_init], inc="vp")

            slot = {}
            for p in range(32):
                jj, ii = owner_of(p)
                slot[p] = jj * 8 + ii

            kld, vld = {}, {}
            pair_lastS = [None] * NPAIR
            pair_lastPV = [None] * NPAIR

            def load_pair(pr):
                b = pr % 2
                ks, vs = [], []
                for r in range(4):
                    ks.append(P.op("sync", lambda e, pr=pr, r=r, b=b: e.dma_start(
                        out=kpair[b][:, r * 1024:(r + 1) * 1024],
                        in_=kp_ag_out[pr].ap()[r * 128:(r + 1) * 128, :]),
                        waits=[ag_k_p[pr], pair_lastS[pr - 2] if pr >= 2 else None],
                        inc="s_ak0" if b == 0 else "s_ak1"))
                    vs.append(P.op("sync", lambda e, pr=pr, r=r, b=b: e.dma_start(
                        out=vpair[b][:, r * 1040:(r + 1) * 1040].rearrange(
                            "p (i c) -> p i c", c=130),
                        in_=v_ag_out.ap()[r * TPC:(r + 1) * TPC,
                                          pr * 130:(pr + 1) * 130].rearrange(
                            "(i q) c -> q i c", q=128)),
                        waits=[ag_v, pair_lastPV[pr - 2] if pr >= 2 else None],
                        inc="s_av0" if b == 0 else "s_av1"))
                kld[pr], vld[pr] = ks, vs

            load_pair(0)
            load_pair(1)

            div_done = {}
            ln_done = {}
            re_done = {}
            r2_last = {}
            bank_war = {}
            pend_s = []
            pend_r = []
            pend_v = []
            prev_pv_tail = [None, None]
            pS_last = [None, None]
            for pr in range(NPAIR):
                b = pr % 2
                for ihead in range(4):
                    t = pr * 4 + ihead
                    tasks = []
                    for kc in range(NI):
                        segs = ([(128 * kc, 512 - 128 * kc), (512, 512)]
                                if kc < 4 else [(128 * kc, 1024 - 128 * kc)])
                        for (c0, ln) in segs:
                            for m in range(4):
                                tasks.append((kc, c0, ln, m))
                    ntask = len(tasks)
                    exps = [None] * ntask
                    pvdep = [[None] * ntask, [None] * ntask]
                    pvs = [[None] * ntask, [None] * ntask]
                    lastSs = [None] * ntask

                    def emit_Smm(n):
                        kc, c0, ln, m = tasks[n]
                        ps_ = pS[n % 2]
                        s_ = slot[4 * kc + m]
                        lastS = None
                        for par in range(2):
                            rb_ = par * 64
                            w = [pS_last[n % 2]]
                            if n == 0:
                                w += kld[pr]
                            lastS = P.op("tensor", lambda e, s_=s_, c0=c0, ln=ln, b=b, rb_=rb_, t=t, par=par, ps_=ps_: e.matmul(
                                ps_[:, par * 512: par * 512 + ln],
                                kpair[b][rb_:rb_ + 64, s_ * 128:(s_ + 1) * 128],
                                qT[rb_:rb_ + 64, t * TPC + c0: t * TPC + c0 + ln],
                                start=True, stop=True),
                                waits=w, inc="tp" if par == 1 else None)
                        lastSs[n] = lastS

                    def emit_exp(n):
                        kc, c0, ln, m = tasks[n]
                        ps_ = pS[n % 2]
                        pt = ptile[n % 2]
                        ptv = pt[:].rearrange("p (q n) -> p q n", q=2)
                        war = (pvs[1][n - 2] if n >= 2 else prev_pv_tail[n])
                        ex = P.op("scalar", lambda e, ln=ln, pt=ptv, t=t, ps_=ps_: e.activation(
                            out=pt[:, :, 0:ln],
                            in_=ps_[:].rearrange("p (q n) -> p q n", q=2)[:, :, 0:ln],
                            func=AF.Exp),
                            waits=[lastSs[n], war], inc="ap")
                        pS_last[n % 2] = ex
                        exps[n] = ex
                        for par in range(2):
                            dep = ex
                            if c0 == 128 * kc:
                                dep = P.op("vector", lambda e, pt=ptv, kc=kc, m=m, par=par: e.tensor_tensor(
                                    out=pt[:, par, 0:128],
                                    in0=pt[:, par, 0:128],
                                    in1=mask_b[:, (kc * 4 + m) * 128:
                                               (kc * 4 + m + 1) * 128],
                                    op=ALU.mult), waits=[ex], inc="vp")
                            pvdep[par][n] = dep

                    def emit_PV(n):
                        kc, c0, ln, m = tasks[n]
                        pt = ptile[n % 2]
                        ptv = pt[:].rearrange("p (q n) -> p q n", q=2)
                        s_ = slot[4 * kc + m]
                        half = 0 if c0 < 512 else 1
                        c0h = c0 - half * 512
                        st = (kc == 0 and m == 0 and c0h == 0)
                        sp = (half == 0 and kc == 3 and m == 3) or \
                             (half == 1 and kc == 7 and m == 3)
                        for par in range(2):
                            ob = pOq[par][half]
                            w = [pvdep[par][n]]
                            if n == 0:
                                w += vld[pr]
                            if st:
                                w.append(bank_war.get((par, half)))
                            pvs[par][n] = P.op("tensor", lambda e, s_=s_, c0h=c0h, ln=ln, b=b, par=par, ob=ob, st=st, sp=sp, pt=ptv, par_=par: e.matmul(
                                ob[0:65, c0h:c0h + ln],
                                vpair[b][:, s_ * 130 + par * 65:
                                         s_ * 130 + (par + 1) * 65],
                                pt[:, par, 0:ln],
                                start=st, stop=sp),
                                waits=w, inc="tp" if par == 1 else None)
                        pair_lastPV[pr] = pvs[1][n]

                    def emit_evac(par, half, pv_label):
                        ob = pOq[par][half]
                        cs = half * 512
                        ev = P.op("vector", lambda e, ob=ob, par=par, cs=cs: e.tensor_copy(
                            out=osb[par][:, cs:cs + 512], in_=ob[0:64, :]),
                            waits=[pv_label, div_done.get((t - 1, par))], inc="vp")
                        dr = P.op("vector", lambda e, ob=ob, par=par, cs=cs: e.tensor_copy(
                            out=rdn[0:1, par * 1024 + cs: par * 1024 + cs + 512],
                            in_=ob[64:65, :]),
                            waits=[ev, re_done.get(t - 1)], inc="vp")
                        bank_war[(par, half)] = dr
                        return ev, dr

                    def mk_ln(t, drs):
                        def go():
                            ln_done[t] = P.op("scalar", lambda e: e.activation(
                                out=rdn[0:1, :], in_=rdn[0:1, :],
                                func=AF.Ln), waits=drs, inc="ap")
                        return go

                    def mk_re(t):
                        def go():
                            re_done[t] = P.op("scalar", lambda e: e.activation(
                                out=rdn[32:33, :], in_=rdn[0:1, :],
                                func=AF.Exp, scale=-1.0),
                                waits=[ln_done[t], r2_last.get(t - 1)], inc="ap")
                        return go

                    def mk_div(t, evs):
                        def go():
                            re = re_done[t]
                            for par in range(2):
                                d = dnb[par]
                                r2a = P.op("vector", lambda e, par=par, d=d: e.tensor_copy(
                                    out=d[0:1, :],
                                    in_=rdn[32:33, par * 1024:(par + 1) * 1024]),
                                    waits=[re, div_done.get((t - 1, par))], inc="vp")
                                r2b = P.op("vector", lambda e, par=par, d=d: e.tensor_copy(
                                    out=d[32:33, :],
                                    in_=rdn[32:33, par * 1024:(par + 1) * 1024]),
                                    waits=[r2a], inc="vp")
                                r2_last[t] = r2b
                                sh = P.op("vector", lambda e, d=d: e.stream_shuffle(
                                    out=d[:], in_=d[:], mask=[0] * 32),
                                    waits=[r2b, rb_init if t == 0 else None], inc="vp")
                                div_done[(t, par)] = P.op("gpsimd", lambda e, t=t, par=par, d=d: e.tensor_tensor(
                                    out=otile[par * 64:par * 64 + 64, t * TPC:(t + 1) * TPC],
                                    in0=osb[par][:], in1=d[:], op=ALU.mult),
                                    waits=[sh, evs[par * 2], evs[par * 2 + 1],
                                           P.cur("pp")], inc="pp")
                        return go

                    idx_low_end = max(n for n, tk in enumerate(tasks) if tk[1] < 512)
                    evd = {}
                    for n in range(ntask):
                        emit_Smm(n)
                        if n >= 2:
                            emit_PV(n - 2)
                        emit_exp(n)
                        if n == 2:
                            for go in pend_s:
                                go()
                            pend_s = []
                        if n == 8:
                            for go in pend_r:
                                go()
                            pend_r = []
                        if n == 14:
                            for go in pend_v:
                                go()
                            pend_v = []
                        if n == idx_low_end + 2:
                            for par in range(2):
                                evd[(par, 0)] = emit_evac(par, 0, pvs[1][idx_low_end])
                    emit_PV(ntask - 2)
                    emit_PV(ntask - 1)
                    pair_lastS[pr] = exps[ntask - 1]
                    prev_pv_tail = [pvs[1][ntask - 2], pvs[1][ntask - 1]]
                    for par in range(2):
                        evd[(par, 1)] = emit_evac(par, 1, pvs[1][ntask - 1])
                    pend_s.append(mk_ln(t, [evd[(p_, h_)][1] for p_ in range(2)
                                            for h_ in range(2)]))
                    pend_r.append(mk_re(t))
                    pend_v.append(mk_div(t, [evd[(0, 0)][0], evd[(0, 1)][0],
                                             evd[(1, 0)][0], evd[(1, 1)][0]]))
                if pr + 2 < NPAIR:
                    load_pair(pr + 2)
            for go in pend_s:
                go()
            for go in pend_r:
                go()
            for go in pend_v:
                go()
        P.barrier()

        with ExitStack() as ph:
            wocol = [SB(ph, "wocolA", [128, NDC * 128], BF16),
                     SB(ph, "wocolB", [128, NDC * 128], BF16)]
            pmean2 = PS(ph, "pmean2", [1, 1024])
            psq2 = PS(ph, "psq2", [1, 1024])
            wo_scope = ExitStack()
            pwo = [PS(wo_scope, "pwoA", [128, 1024]), PS(wo_scope, "pwoB", [128, 1024])]
            womm = [None] * NDC
            tile_free = [None] * NDC
            spills = [None] * NDC
            last_stat = None
            stats_q = []

            def emit_stats(dmb, cst, sqr):
                nonlocal last_stat
                for h in range(2):
                    P.op("tensor", lambda e, dmb=dmb, h=h: e.matmul(
                        pmean2[0:1, h * 512:(h + 1) * 512], onescol_b[:],
                        xbtile[:, h * 512:(h + 1) * 512],
                        start=(dmb == 0), stop=(dmb == NDC - 1)), waits=[cst])
                    last_stat = P.op("tensor", lambda e, dmb=dmb, h=h: e.matmul(
                        psq2[0:1, h * 512:(h + 1) * 512], onescol_b[:],
                        xsqb[:, h * 512:(h + 1) * 512],
                        start=(dmb == 0), stop=(dmb == NDC - 1)),
                        waits=[sqr], inc="tp" if h == 1 else None)

            for dmb in range(NDC):
                wol = [P.op("gpsimd", lambda e, b=wocol[dmb % 2], c=dmb: e.dma_start(
                    out=b[:], in_=wo_t[c * 128:(c + 1) * 128, :]),
                    waits=[womm[dmb - 2] if dmb >= 2 else None],
                    inc="g_wo0" if dmb % 2 == 0 else "g_wo1")]
                pp = pwo[dmb % 2]
                for hdc in range(NDC):
                    for h in range(2):
                        womm[dmb] = P.op("tensor", lambda e, b=wocol[dmb % 2], p=pp, hdc=hdc, h=h: e.matmul(
                            p[:, h * 512:(h + 1) * 512], b[:, hdc * 128:(hdc + 1) * 128],
                            otile[:, hdc * TPC + h * 512: hdc * TPC + (h + 1) * 512],
                            start=(hdc == 0), stop=(hdc == NDC - 1)),
                            waits=wol + [tile_free[dmb - 2] if dmb >= 2 else None],
                            inc="tp" if (hdc == NDC - 1 and h == 1) else None)
                if stats_q:
                    emit_stats(*stats_q.pop(0))
                sl = dmb % 2
                xt = xtile[:, sl * 1024:(sl + 1) * 1024]
                xl = P.op("sync", lambda e, d=xt, dmb=dmb: e.dma_start(
                    out=d, in_=xT_in[dmb * 128:(dmb + 1) * 128, :]),
                    waits=[tile_free[dmb - 2] if dmb >= 2 else None,
                           spills[dmb - 2] if dmb >= 2 else None],
                    inc="s_x0" if sl == 0 else "s_x1")
                x2 = P.op("vector", lambda e, d=xt, p=pp: e.tensor_tensor(
                    out=d, in0=d, in1=p[:], op=ALU.add), waits=[xl, womm[dmb]], inc="vp")
                spills[dmb] = P.op("sync", lambda e, s=xt, dmb=dmb: e.dma_start(
                    out=x2spill.ap()[dmb * 128:(dmb + 1) * 128, :], in_=s),
                    waits=[x2], inc="s_sp0" if sl == 0 else "s_sp1")
                cst = P.op("vector", lambda e, s=xt: e.tensor_copy(out=xbtile[:], in_=s),
                           waits=[x2, last_stat], inc="vp")
                sqr = P.op("vector", lambda e: e.tensor_tensor(
                    out=xsqb[:], in0=xbtile[:], in1=xbtile[:], op=ALU.mult),
                    waits=[cst], inc="vp")
                stats_q.append((dmb, cst, sqr))
                tile_free[dmb] = sqr
            while stats_q:
                emit_stats(*stats_q.pop(0))
            wo_scope.close()
            ln2_done = ln_rows_and_apply(ph, "L2", pmean2, psq2, last_stat,
                                         x2spill.ap(),
                                         [P.cur("s_sp0"), P.cur("s_sp1")])

            ptr = PS(ph, "ptr", [128, 64])
            lgsb = SB(ph, "lgsb", [128, 64])
            lmm = None
            for tb in range(8):
                for dc in range(NDC):
                    lmm = P.op("tensor", lambda e, tb=tb, dc=dc: e.matmul(
                        ptr[:, tb * 8:(tb + 1) * 8],
                        xlnT[:, dc * TPC + tb * 128: dc * TPC + (tb + 1) * 128],
                        rt_b[:, dc * 8:(dc + 1) * 8],
                        start=(dc == 0), stop=(dc == NDC - 1)),
                        waits=[ln2_done[dc]],
                        inc="tp" if dc == NDC - 1 else None)
            lev = P.op("scalar", lambda e: e.copy(out=lgsb[:], in_=ptr[:]),
                       waits=[lmm], inc="ap")
            lgst = P.op("sync", lambda e: e.dma_start(
                out=lg_ag_in.ap().rearrange("(t p) e -> p t e", p=128),
                in_=lgsb[:].rearrange("p (t e) -> p t e", e=8)),
                waits=[lev], inc="s_lg")
            ag_lg = P.op("gpsimd", lambda e: e.collective_compute(
                "AllGather", ALU.bypass, replica_groups=[list(range(8))],
                ins=[lg_ag_in.ap().opt()], outs=[lg_ag_out.ap().opt()]),
                waits=[lgst], inc="ccs")
        P.barrier(skip=("ccs",))
        att.close()

        with ExitStack() as ph:
            r_l = [SB(ph, "r_lA", [128, 8]), SB(ph, "r_lB", [128, 8])]
            r_v = [SB(ph, "r_vA", [128, 8]), SB(ph, "r_vB", [128, 8])]
            r_e = SB(ph, "r_e", [128, 8])
            r_s = SB(ph, "r_s", [128, 24])
            bs_strip = SB(ph, "bs_strip", [1, 1024])
            bs64 = SB(ph, "bs64", [64, 16])
            tot_s = SB(ph, "tot_s", [1, 8])
            offrow = SB(ph, "offrow", [1, 8])
            wtrow = SB(ph, "wtrow", [1, 1024])
            ppos = PS(ph, "ppos", [128, 8])
            pbs1 = PS(ph, "pbs1", [1, 8])
            poff = PS(ph, "poff", [64, 8])
            pwt = PS(ph, "pwt", [128, 1024])
            pwtT = PS(ph, "pwtT", [8, 128])

            gb_src = {}
            for gb in range(NGB):
                bb, p = gb // 32, gb % 32
                jj, ii = owner_of(p)
                gb_src[gb] = (4 * bb + jj) * TPC + ii * 128

            lgl = [None, None]
            mmb_prev = [None, None]
            bse_prev = None
            oh_done = [None] * NGB
            for gb in range(NGB):
                s2 = gb % 2
                lgl[s2] = P.op("sync", lambda e, gb=gb, s2=s2: e.dma_start(
                    out=r_l[s2][:], in_=lg_ag_out.ap()[gb_src[gb]:gb_src[gb] + 128, :]),
                    waits=[ag_lg, oh_done[gb - 2] if gb >= 2 else None],
                    inc="s_rt0" if s2 == 0 else "s_rt1")
                mx = P.op("vector", lambda e, s2=s2: e.max(r_v[s2][:], r_l[s2][:]),
                          waits=[lgl[s2]], inc="vp")
                o1 = P.op("vector", lambda e, gb=gb, s2=s2: e.tensor_scalar(
                    out=ohall[:, gb * 16: gb * 16 + 8], in0=r_l[s2][:],
                    scalar1=r_v[s2][:, 0:1], scalar2=None, op0=ALU.is_equal),
                    waits=[mx], inc="vp")
                o2 = P.op("vector", lambda e, gb=gb, s2=s2: e.tensor_scalar(
                    out=ohall[:, gb * 16 + 8: gb * 16 + 16], in0=r_l[s2][:],
                    scalar1=r_v[s2][:, 1:2], scalar2=None, op0=ALU.is_equal),
                    waits=[mx], inc="vp")
                oh_done[gb] = o2
            cnt_mm = None
            for h in range(2):
                cnt_mm = P.op("tensor", lambda e, h=h: e.matmul(
                    pwt[0:1, h * 512:(h + 1) * 512], onescol_s[:],
                    ohall[:, h * 512:(h + 1) * 512], start=True, stop=True),
                    waits=[oh_done[NGB - 1]], inc="tp")
            bse_prev = P.op("scalar", lambda e: e.copy(
                out=bs_strip[:], in_=pwt[0:1, :]), waits=[cnt_mm], inc="ap")
            bsd = P.op("sync", lambda e: e.dma_start(
                out=scr_dram.ap()[4:5, 0:1024], in_=bs_strip[:]),
                waits=[bse_prev], inc="s_strip")
            bsr = P.op("sync", lambda e: e.dma_start(
                out=bs64[:].rearrange("p (k e) -> p k e", e=8),
                in_=scr_dram.ap()[4:5, 0:1024].rearrange(
                    "o (b k e) -> (o b) k e", k=2, e=8)), waits=[bsd], inc="s_strip")
            om1 = P.op("tensor", lambda e: e.matmul(
                poff[:], ut64s_s[:], bs64[:, 0:8], start=True, stop=True),
                waits=[bsr], inc="tp")
            ot1 = P.op("tensor", lambda e: e.matmul(
                ppos[0:1, :], onescol_s[0:64, :], bs64[:, 0:8], start=True, stop=True),
                waits=[bsr, P.cur("ap")], inc="tp")
            ote = P.op("scalar", lambda e: e.copy(out=tot_s[:], in_=ppos[0:1, :]),
                       waits=[ot1], inc="ap")
            oe1 = P.op("scalar", lambda e: e.copy(out=bs64[:, 0:8], in_=poff[:]),
                       waits=[om1, ot1], inc="ap")
            om2a = P.op("tensor", lambda e: e.matmul(
                poff[:], ut64s_s[:], bs64[:, 8:16], start=True, stop=False),
                waits=[oe1], inc="tp")
            om2b = P.op("tensor", lambda e: e.matmul(
                poff[:], ones_s[0:1, 0:64], tot_s[:], start=False, stop=True),
                waits=[ote], inc="tp")
            oe2 = P.op("scalar", lambda e: e.copy(out=bs64[:, 8:16], in_=poff[:]),
                       waits=[om2b], inc="ap")

            wt_done = []
            own_prev = None
            for i in range(NI):
                s2 = 0
                lgo = P.op("sync", lambda e, i=i: e.dma_start(
                    out=r_l[0][:], in_=lg_ag_in.ap()[i * 128:(i + 1) * 128, :]),
                    waits=[ag_lg, own_prev if own_prev else oh_done[NGB - 2]],
                    inc="s_rt0")
                mx = P.op("vector", lambda e: e.max(r_v[0][:], r_l[0][:]),
                          waits=[lgo, oh_done[NGB - 1]], inc="vp")
                ng = P.op("vector", lambda e: e.tensor_scalar(
                    out=r_s[:, 0:1], in0=r_v[0][:, 0:1], scalar1=-1.0, scalar2=None,
                    op0=ALU.mult), waits=[mx], inc="vp")
                ee = P.op("scalar", lambda e: e.activation(
                    out=r_e[:], in_=r_l[0][:], func=AF.Exp, bias=r_s[:, 0:1]),
                    waits=[ng, bse_prev], inc="ap")
                dn = P.op("vector", lambda e: e.tensor_reduce(
                    r_s[:, 1:2], r_e[:], axis=AX.X, op=ALU.add), waits=[ee], inc="vp")
                p1 = P.op("vector", lambda e: e.reciprocal(r_s[:, 2:3], r_s[:, 1:2]),
                          waits=[dn], inc="vp")
                e2_ = P.op("scalar", lambda e: e.activation(
                    out=r_s[:, 3:4], in_=r_v[0][:, 1:2], func=AF.Exp, bias=r_s[:, 0:1]),
                    waits=[p1], inc="ap")
                p2 = P.op("vector", lambda e, i=i: e.tensor_tensor(
                    out=wtp[:, 2 * i + 1:2 * i + 2], in0=r_s[:, 3:4], in1=r_s[:, 2:3],
                    op=ALU.mult), waits=[e2_], inc="vp")
                p1c = P.op("vector", lambda e, i=i: e.tensor_copy(
                    out=wtp[:, 2 * i:2 * i + 1], in_=r_s[:, 2:3]), waits=[p1], inc="vp")
                o1 = P.op("vector", lambda e: e.tensor_scalar(
                    out=r_s[:, 8:16], in0=r_l[0][:], scalar1=r_v[0][:, 0:1],
                    scalar2=None, op0=ALU.is_equal), waits=[mx], inc="vp")
                o2 = P.op("vector", lambda e: e.tensor_scalar(
                    out=r_s[:, 16:24], in0=r_l[0][:], scalar1=r_v[0][:, 1:2],
                    scalar2=None, op0=ALU.is_equal), waits=[mx], inc="vp")
                kcol = [None, None]
                sel_prev = None
                for kk in range(2):
                    osel = P.op("tensor", lambda e, i=i, kk=kk: e.matmul(
                        pbs1[:], ownsel_s[:, i:i + 1], bs64[:, 8 * kk:8 * kk + 8],
                        start=True, stop=True),
                        waits=[oe1 if kk == 0 else oe2, P.cur("ap")], inc="tp")
                    ose = P.op("scalar", lambda e: e.copy(out=offrow[:], in_=pbs1[:]),
                               waits=[osel], inc="ap")
                    m1 = P.op("tensor", lambda e, kk=kk: e.matmul(
                        ppos[:], ut128_s[:], r_s[:, 8 + 8 * kk:16 + 8 * kk],
                        start=True, stop=False), waits=[o1 if kk == 0 else o2,
                                                        sel_prev, ote], inc="tp")
                    m2 = P.op("tensor", lambda e: e.matmul(
                        ppos[:], ones_s[:], offrow[:], start=False, stop=True),
                        waits=[ose], inc="tp")
                    kp = P.op("vector", lambda e: e.tensor_scalar(
                        out=r_e[:], in0=ppos[:], scalar1=float(CAP), scalar2=None,
                        op0=ALU.is_le), waits=[m2], inc="vp")
                    sel = P.op("vector", lambda e, kk=kk: e.tensor_tensor(
                        out=r_e[:], in0=r_e[:], in1=r_s[:, 8 + 8 * kk:16 + 8 * kk],
                        op=ALU.mult), waits=[kp], inc="vp")
                    kcol[kk] = P.op("vector", lambda e, kk=kk: e.tensor_reduce(
                        r_s[:, 4 + kk:5 + kk], r_e[:], axis=AX.X, op=ALU.add),
                        waits=[sel], inc="vp")
                    sel_prev = kcol[kk]
                t1 = P.op("vector", lambda e, i=i: e.tensor_scalar(
                    out=r_s[:, 6:7], in0=r_s[:, 4:5], scalar1=wtp[:, 2 * i:2 * i + 1],
                    scalar2=None, op0=ALU.mult), waits=[kcol[0], p1c], inc="vp")
                wd = P.op("vector", lambda e, i=i: e.scalar_tensor_tensor(
                    out=wt_s[:, i:i + 1], in0=r_s[:, 5:6],
                    scalar=wtp[:, 2 * i + 1:2 * i + 2], in1=r_s[:, 6:7],
                    op0=ALU.mult, op1=ALU.add), waits=[t1, p2, kcol[1]], inc="vp")
                wt_done.append(wd)
                own_prev = wd
            wtt = P.op("tensor", lambda e: e.transpose(
                pwtT[:], wt_s[:], eye_s[:]), waits=wt_done, inc="tp")
            wte = P.op("scalar", lambda e: e.copy(out=mbc[0:8, 0:128], in_=pwtT[:]),
                       waits=[wtt], inc="ap")
            wtd1 = P.op("sync", lambda e: e.dma_start(
                out=scr_dram.ap()[7:8, 0:1024].rearrange("o (p q) -> (o p) q", p=8),
                in_=mbc[0:8, 0:128]), waits=[wte], inc="s_wt")
            wtd2 = P.op("sync", lambda e: e.dma_start(
                out=wtrow[:], in_=scr_dram.ap()[7:8, 0:1024]), waits=[wtd1], inc="s_wt")
            wbm = None
            for h in range(2):
                wbm = P.op("tensor", lambda e, h=h: e.matmul(
                    pwt[:, h * 512:(h + 1) * 512], ones_s[:],
                    wtrow[0:1, h * 512:(h + 1) * 512], start=True, stop=True),
                    waits=[wtd2], inc="tp")
            wtb_done = P.op("scalar", lambda e: e.copy(out=wtbc[:], in_=pwt[:]),
                            waits=[wbm], inc="ap")
        P.barrier()

        with ExitStack() as ph:
            h_sb = SB(ph, "h_sb", [128, 8 * TPC], BF16)
            w1fb = [SB(ph, "w1fbA", [128, NDC * 128], BF16),
                    SB(ph, "w1fbB", [128, NDC * 128], BF16)]
            w2fb = SB(ph, "w2fb", [128, 8 * D], BF16)
            outp = SB(ph, "outp", [128, NDC * TPC])
            ph1 = [PS(ph, "ph1A", [128, TPC]), PS(ph, "ph1B", [128, TPC])]
            po2 = [PS(ph, "po2A", [128, TPC]), PS(ph, "po2B", [128, TPC])]

            h1mm = {}
            gel = {}
            w2l = {}
            o2mm = {}
            oadd = {}
            epi_prev = [None, None]
            for sc in range(8):
                for fb in range(8):
                    fbg = sc * 8 + fb
                    w1l = P.op("gpsimd", lambda e, b=w1fb[fb % 2], c=fbg: e.dma_start(
                        out=b[:], in_=w1_t[c * 128:(c + 1) * 128, :]),
                        waits=[h1mm.get((sc * 8 + fb - 2))],
                        inc="g_w10" if fb % 2 == 0 else "g_w11")
                    w2l[fbg] = P.op("gpsimd", lambda e, fb=fb, fbg=fbg: e.dma_start(
                        out=w2fb[:, fb * D:(fb + 1) * D],
                        in_=w2_t[fbg * 128:(fbg + 1) * 128, :]),
                        waits=[o2mm.get(sc - 1) if fb == 0 else None], inc="g_w2")
                    pp = ph1[fb % 2]
                    mm = None
                    for dc in range(NDC):
                        for h in range(2):
                            mm = P.op("tensor", lambda e, b=w1fb[fb % 2], p=pp, dc=dc, h=h: e.matmul(
                                p[:, h * 512:(h + 1) * 512], b[:, dc * 128:(dc + 1) * 128],
                                xlnT[:, dc * TPC + h * 512: dc * TPC + (h + 1) * 512],
                                start=(dc == 0), stop=(dc == NDC - 1)),
                                waits=[w1l, gel.get(sc * 8 + fb - 2)],
                                inc="tp" if (dc == NDC - 1 and h == 1) else None)
                    h1mm[fbg] = mm
                    gel[fbg] = P.op("scalar", lambda e, p=pp, fb=fb, fbg=fbg: e.activation(
                        out=h_sb[:, fb * TPC:(fb + 1) * TPC], in_=p[:], func=AF.Gelu,
                        bias=b1_s[:, fbg:fbg + 1]),
                        waits=[mm, o2mm.get(sc - 1) if sc >= 1 else None], inc="ap")
                lastmm = None
                for dmb in range(NDC):
                    pp = po2[dmb % 2]
                    for fb in range(8):
                        for h in range(2):
                            lastmm = P.op("tensor", lambda e, pp=pp, fb=fb, dmb=dmb, h=h: e.matmul(
                                pp[:, h * 512:(h + 1) * 512],
                                w2fb[:, fb * D + dmb * 128: fb * D + (dmb + 1) * 128],
                                h_sb[:, fb * TPC + h * 512: fb * TPC + (h + 1) * 512],
                                start=(fb == 0), stop=(fb == 7)),
                                waits=[gel[sc * 8 + fb], w2l[sc * 8 + 7],
                                       oadd.get((sc, dmb - 2)) if dmb >= 2
                                       else oadd.get((sc - 1, NDC - 2 + dmb))],
                                inc="tp" if (fb == 7 and h == 1) else None)
                    if sc == 0:
                        oadd[(sc, dmb)] = P.op("vector", lambda e, pp=pp, dmb=dmb: e.tensor_scalar(
                            out=outp[:, dmb * TPC:(dmb + 1) * TPC], in0=pp[:],
                            scalar1=b2_s[:, dmb:dmb + 1], scalar2=None, op0=ALU.add),
                            waits=[lastmm], inc="vp")
                    else:
                        oadd[(sc, dmb)] = P.op("vector", lambda e, pp=pp, dmb=dmb: e.tensor_tensor(
                            out=outp[:, dmb * TPC:(dmb + 1) * TPC],
                            in0=outp[:, dmb * TPC:(dmb + 1) * TPC], in1=pp[:],
                            op=ALU.add), waits=[lastmm], inc="vp")
                    if sc == 7:
                        dc = dmb
                        sl = dc % 2
                        xt = xtile[:, sl * 1024:(sl + 1) * 1024]
                        ld = P.op("sync", lambda e, d=xt, dc=dc: e.dma_start(
                            out=d, in_=x2spill.ap()[dc * 128:(dc + 1) * 128, :]),
                            waits=[epi_prev[sl]], inc="s_x0" if sl == 0 else "s_x1")
                        t1 = P.op("vector", lambda e, dc=dc: e.tensor_tensor(
                            out=mbc[:], in0=outp[:, dc * TPC:(dc + 1) * TPC],
                            in1=wtbc[:], op=ALU.mult),
                            waits=[oadd[(7, dc)], ld, wtb_done], inc="vp")
                        t2 = P.op("vector", lambda e, s=xt: e.tensor_tensor(
                            out=s, in0=s, in1=mbc[:], op=ALU.add), waits=[t1], inc="vp")
                        st = P.op("sync", lambda e, s=xt, dc=dc: e.dma_start(
                            out=xoutT[dc * 128:(dc + 1) * 128, :], in_=s),
                            waits=[t2], inc="s_out0" if sl == 0 else "s_out1")
                        epi_prev[sl] = st
                o2mm[sc] = lastmm
            outs_done = [P.cur("s_out0"), P.cur("s_out1")]
            P.op("sync", None, waits=outs_done)
            P.op("gpsimd", lambda e: e.drain(), waits=outs_done, inc="gp")
            P.op("gpsimd", None, waits=[P.cur("gp")])

        P.emit(block, sems)
    return nc



def _prep_inputs(inputs):
    x = np.asarray(inputs["L0_attn_x"])
    Wq = np.asarray(inputs["L0_attn_Wq"])
    Wk = np.asarray(inputs["L0_attn_Wk"])
    Wv = np.asarray(inputs["L0_attn_Wv"])
    Wo = np.asarray(inputs["L0_attn_Wo"])
    rw = np.asarray(inputs["L0_ffn_router_w"])
    w1 = np.asarray(inputs["L0_ffn_w1"])
    b1 = np.asarray(inputs["L0_ffn_b1"])
    w2 = np.asarray(inputs["L0_ffn_w2"])
    b2 = np.asarray(inputs["L0_ffn_b2"])

    hdperm = np.concatenate([np.arange(0, 64, 2), np.arange(1, 64, 2)])
    qperm = np.zeros(D, np.int64)
    for p in range(4):
        for i2 in range(4):
            t = 4 * p + i2
            gA = 8 * p + i2
            gB = 8 * p + 4 + i2
            qperm[t * 128: t * 128 + 64] = gA * 64 + hdperm
            qperm[t * 128 + 64: t * 128 + 128] = gB * 64 + hdperm
    kperm = np.zeros(512, np.int64)
    for kv in range(8):
        kperm[kv * 64:(kv + 1) * 64] = kv * 64 + hdperm
    operm = np.zeros(D, np.int64)
    for p in range(4):
        for i2 in range(4):
            t = 4 * p + i2
            gA = 8 * p + i2
            gB = 8 * p + 4 + i2
            operm[t * 128: t * 128 + 64] = gA * 64 + np.arange(64)
            operm[t * 128 + 64: t * 128 + 128] = gB * 64 + np.arange(64)

    def tilemaj(a):
        dd, nc_ = a.shape
        nt = nc_ // 128
        out = np.empty((nt * 128, dd), a.dtype)
        for t in range(nt):
            out[t * 128:(t + 1) * 128, :] = (
                a[:, t * 128:(t + 1) * 128].reshape(dd // 128, 128, 128)
                .transpose(1, 0, 2).reshape(128, dd))
        return np.ascontiguousarray(out)

    shared = {
        "wq_t": tilemaj(Wq.T[:, qperm]),
        "wk_t": tilemaj(Wk.T[:, kperm]),
        "wv_t": np.ascontiguousarray(Wv.T),
        "wo_t": tilemaj(Wo.T[operm, :]),
        "rt_t": np.ascontiguousarray(
            rw.T.reshape(NDC, 128, E).transpose(1, 0, 2).reshape(128, NDC * E)),
        "w1_t": tilemaj(w1.T),
        "b1_t": np.ascontiguousarray(b1.reshape(NFB, 128).T),
        "w2_t": np.ascontiguousarray(w2.T),
        "b2_t": np.ascontiguousarray(b2.reshape(NDC, 128).T),
        "ut128": np.ascontiguousarray(
            (np.arange(128)[:, None] <= np.arange(128)[None, :]).astype(np.float32)),
        "ut64s": np.ascontiguousarray(
            (np.arange(64)[:, None] < np.arange(64)[None, :]).astype(np.float32)),
        "eye_f": np.eye(128, dtype=np.float32),
        "ones_r": np.ones((1, 128), np.float32),
        "onescol": np.ones((128, 1), np.float32),
    }

    inv = (1.0 / 10000.0 ** (np.arange(0, HD, 2, dtype=np.float64) / HD))
    in_maps = []
    for c in range(8):
        b, j = divmod(c, 4)
        posb = np.concatenate([np.arange(pos_of(i, j) * 128, pos_of(i, j) * 128 + 128)
                               for i in range(NI)])
        xo = np.ascontiguousarray(x[b, posb, :].T)
        f = posb[:, None].astype(np.float64) * inv[None, :]
        cos = np.cos(f).T.astype(np.float32)
        sin = np.sin(f).T.astype(np.float32)
        c2 = np.concatenate([cos, cos, cos, cos], 0)
        s2 = np.concatenate([-sin, sin, -sin, sin], 0)
        mask = np.zeros((128, NI * 4 * 128), np.float32)
        for i in range(NI):
            mi = (i + j) % 4
            for m in range(4):
                blk = mask[:, (i * 4 + m) * 128:(i * 4 + m) * 128 + 128]
                if m < mi:
                    blk[:] = 1.0
                elif m == mi:
                    blk[:] = (np.arange(128)[:, None] <= np.arange(128)[None, :])
        osel = np.zeros((64, 8), np.float32)
        for i in range(NI):
            gb = b * 32 + pos_of(i, j)
            osel[gb, i] = 1.0
        im = dict(shared)
        im.update({
            "xT_in": xo,
            "cq2": np.ascontiguousarray(c2 * SCALE),
            "sq2": np.ascontiguousarray(s2 * SCALE),
            "ck2": np.ascontiguousarray(c2),
            "sk2": np.ascontiguousarray(s2),
            "mask_in": mask,
            "ownsel": osel,
        })
        in_maps.append(im)
    return in_maps


_NC_CACHE = {}


def kernel(**inputs):
    in_maps = _prep_inputs(inputs)
    if "nc" not in _NC_CACHE:
        _NC_CACHE["nc"] = build_nc()
    nc = _NC_CACHE["nc"]
    res = run_bass_kernel_spmd(nc, in_maps, core_ids=list(range(8)))
    out = np.zeros((B, L, D), np.float32)
    for c in range(8):
        b, j = divmod(c, 4)
        xt = res.results[c]["xoutT"]
        for i in range(NI):
            p = pos_of(i, j)
            out[b, p * 128:(p + 1) * 128, :] = xt[:, i * 128:(i + 1) * 128].T
    return out


# version 32 (final)
# speedup vs baseline: 1.1741x; 1.1741x over previous
import numpy as np
import ml_dtypes
from contextlib import ExitStack

import concourse.bass as bass
import concourse.mybir as mybir
from concourse.bass_utils import run_bass_kernel_spmd

F32 = mybir.dt.float32
BF16 = mybir.dt.bfloat16
AF = mybir.ActivationFunctionType
ALU = mybir.AluOpType
AX = mybir.AxisListType

B, L, D = 2, 4096, 2048
HKV, HD = 8, 64
SCALE = HD ** (-0.5)
CAP = 512
E = 8
FF = 8192
TPC = 1024
NI = 8
NDC = D // 128
NQB = 16
NPAIR = 4
NFB = FF // 128
NGB = (B * L) // 128
EPS = 1e-5
KV_ROWS = 4 * 128


def pos_of(i, j):
    return 4 * i + (i + j) % 4


def owner_of(p):
    i = p // 4
    j = (p % 4 - i) % 4
    return j, i


ENGINES = ("sync", "gpsimd", "tensor", "vector", "scalar")
DMA_SEM_NAMES = (
    "s_const", "s_x0", "s_x1", "s_x2", "s_x3", "s_kstg", "s_vstg",
    "s_ak0", "s_ak1", "s_av0", "s_av1", "s_lg", "s_rt0", "s_rt1",
    "s_strip", "s_sp0", "s_sp1", "s_wt", "s_out0", "s_out1", "s_swp",
    "s_wl", "s_wl2", "s_wl3",
    "g_const", "g_tiny", "g_wc0", "g_wc1", "g_wv", "g_wo0", "g_wo1",
    "g_w10", "g_w11", "g_w2")


class Prog:
    def __init__(self):
        self.ops = []
        self.ctr = {}

    def op(self, eng, fn, waits=(), inc=None):
        ws = [(w[0], w[1]) for w in waits if w is not None]
        if fn is not None and eng == "vector" and self.ctr.get("vp", 0) > 0:
            ws.append(("vp", self.ctr["vp"]))
        if fn is not None and eng == "scalar" and self.ctr.get("ap", 0) > 0:
            ws.append(("ap", self.ctr["ap"]))
        lbl = None
        if inc is not None:
            amt = 16 if (inc.startswith("s_") or inc.startswith("g_")) else 1
            self.ctr[inc] = self.ctr.get(inc, 0) + amt
            lbl = (inc, self.ctr[inc], amt)
        self.ops.append((eng, fn, ws, lbl))
        return (lbl[0], lbl[1]) if lbl else None

    def cur(self, sem):
        return (sem, self.ctr[sem]) if self.ctr.get(sem, 0) > 0 else None

    def barrier(self, skip=()):
        lbls = []
        for eng in ("gpsimd", "tensor", "vector", "scalar"):
            lbls.append(self.op(eng, lambda e: e.drain(), inc="b" + eng[0]))
        for s in DMA_SEM_NAMES + ("ccs",):
            if s in skip:
                continue
            c = self.cur(s)
            if c:
                lbls.append(c)
        for eng in ENGINES:
            self.op(eng, None, waits=lbls)
        return lbls

    def emit(self, block, sems):
        for eng_name in ENGINES:
            def section(e, _eng=eng_name):
                seen = {}
                for (en, fn, waits, lbl) in self.ops:
                    if en != _eng:
                        continue
                    byname = {}
                    for (s, v) in waits:
                        byname[s] = max(byname.get(s, 0), v)
                    for s, v in byname.items():
                        if v > seen.get(s, 0):
                            e.wait_ge(sems[s], v)
                            seen[s] = v
                    if fn is None:
                        continue
                    inst = fn(e)
                    if lbl is not None:
                        inst.then_inc(sems[lbl[0]], lbl[2])
            getattr(block, eng_name)(section)


def build_nc():
    nc = bass.Bass("TRN2", target_bir_lowering=False)
    P = Prog()

    def din(name, shape, dt=F32):
        return nc.dram_tensor(name, shape, dt, kind="ExternalInput").ap()

    xT_in = din("xT_in", [D, TPC])
    wq_t = din("wq_t", [NQB * 128, D])
    wk_t = din("wk_t", [NPAIR * 128, D])
    wv_t = din("wv_t", [D, HKV * HD])
    wo_t = din("wo_t", [NDC * 128, D])
    rt_t = din("rt_t", [128, NDC * E])
    w1_t = din("w1_t", [NFB * 128, D])
    b1_t = din("b1_t", [128, NFB])
    w2_t = din("w2_t", [FF, D])
    b2_t = din("b2_t", [128, NDC])
    cq2 = din("cq2", [128, TPC])
    sq2 = din("sq2", [128, TPC])
    ck2 = din("ck2", [128, TPC])
    sk2 = din("sk2", [128, TPC])
    mask_in = din("mask_in", [128, NI * 4 * 128])
    ut128 = din("ut128", [128, 128])
    ut64s = din("ut64s", [64, 64])
    eye_f = din("eye_f", [128, 128])
    ones_r = din("ones_r", [1, 128])
    onescol = din("onescol", [128, 1])
    ownsel = din("ownsel", [64, 8])

    xoutT = nc.dram_tensor("xoutT", [D, TPC], F32, kind="ExternalOutput").ap()

    kp_ag_in = [nc.dram_tensor(f"kp_ag_in{p}", [128, TPC], BF16) for p in range(4)]
    kp_ag_out = [nc.dram_tensor(f"kp_ag_out{p}", [512, TPC], BF16) for p in range(4)]
    v_ag_in = nc.dram_tensor("v_ag_in", [TPC, 8 * 65], BF16)
    v_ag_out = nc.dram_tensor("v_ag_out", [4 * TPC, 8 * 65], BF16)
    lg_ag_in = nc.dram_tensor("lg_ag_in", [TPC, E], F32)
    lg_ag_out = nc.dram_tensor("lg_ag_out", [8 * TPC, E], F32)
    scr_dram = nc.dram_tensor("scr_dram", [8, 1024], F32)
    x2spill = nc.dram_tensor("x2spill", [D, TPC], F32)

    root = ExitStack()

    def SB(stack, name, shape, dt=F32):
        return stack.enter_context(nc.sbuf_tensor(name, shape, dt))

    def PS(stack, name, shape, dt=F32):
        return stack.enter_context(nc.psum_tensor(name, shape, dt))

    with root:
        xlnT = SB(root, "xlnT", [128, NDC * TPC], BF16)
        ut128_s = SB(root, "ut128_s", [128, 128])
        ut64s_s = SB(root, "ut64s_s", [64, 64])
        eye_s = SB(root, "eye_s", [128, 128])
        ones_s = SB(root, "ones_s", [1, 128])
        onescol_s = SB(root, "onescol_s", [128, 1])
        ones_b = SB(root, "ones_b", [1, 128], BF16)
        onescol_b = SB(root, "onescol_b", [128, 1], BF16)
        ownsel_s = SB(root, "ownsel_s", [64, 8])
        b1_s = SB(root, "b1_s", [128, NFB])
        b2_s = SB(root, "b2_s", [128, NDC])
        rt_b = SB(root, "rt_b", [128, NDC * E], BF16)
        rowsc = SB(root, "rowsc", [1, 2048])
        mbc = SB(root, "mbc", [128, 1024])
        rbc = SB(root, "rbc", [128, 1024])
        xsqb = SB(root, "xsqb", [128, 1024], BF16)
        xtile = SB(root, "xtile", [128, 4 * 1024])
        xbtile = SB(root, "xbtile", [128, 1024], BF16)
        wtbc = SB(root, "wtbc", [128, 1024])
        wt_s = SB(root, "wt_s", [128, 8])
        wtp = SB(root, "wtp", [128, 16])
        ohall = SB(root, "ohall", [128, NGB * 16])
        att = ExitStack()
        qT = SB(att, "qT", [128, NQB * TPC], BF16)
        otile = SB(att, "otile", [128, NQB * TPC], BF16)
        mask_b = SB(att, "mask_b", [128, NI * 4 * 128], BF16)
        ropestk = ExitStack()
        cq_s = SB(ropestk, "cq_s", [128, TPC], BF16)
        sq_s = SB(ropestk, "sq_s", [128, TPC], BF16)
        ck_s = SB(ropestk, "ck_s", [128, TPC], BF16)
        sk_s = SB(ropestk, "sk_s", [128, TPC], BF16)

        sems = {}
        for s in DMA_SEM_NAMES + ("ccs", "tp", "vp", "ap", "gp", "pp",
                                  "bg", "bt", "bv", "bs"):
            sems[s] = root.enter_context(nc.semaphore(s))

        block = root.enter_context(nc.Block())

        cl = None
        for dst, src in ((ut128_s, ut128), (ut64s_s, ut64s), (eye_s, eye_f),
                         (ones_s, ones_r), (onescol_s, onescol), (b1_s, b1_t),
                         (b2_s, b2_t), (ownsel_s, ownsel)):
            cl = P.op("sync", lambda e, d=dst, s=src: e.dma_start(out=d[:], in_=s),
                      inc="s_const")
        cg = None
        for dst, src in ((ones_b, ones_r), (onescol_b, onescol)):
            cg = P.op("gpsimd", lambda e, d=dst, s=src: e.dma_start(out=d[:], in_=s),
                      inc="g_tiny")
        CONST_TINY = cg
        for dst, src in ((ck_s, ck2), (sk_s, sk2), (cq_s, cq2), (sq_s, sq2),
                         (mask_b, mask_in)):
            cg = P.op("gpsimd", lambda e, d=dst, s=src: e.dma_start(out=d[:], in_=s),
                      inc="g_const")
        cg = P.op("gpsimd", lambda e: e.dma_start(
            out=rt_b[:], in_=rt_t), inc="g_const")
        CONST_L, CONST_G = cl, cg

        def ln_rows_and_apply(stack, tag, pmean, psq, stats_done, xsrc, extra_ld_dep,
                              sb_src=None):
            lst = ExitStack()
            pb = PS(lst, "pb" + tag, [128, 1024])
            pb2 = PS(lst, "pb2" + tag, [128, 1024])
            r1 = P.op("vector", lambda e: e.tensor_scalar(
                out=rowsc[0:1, 0:1024], in0=pmean[0:1, :], scalar1=-1.0 / D,
                scalar2=None, op0=ALU.mult), waits=[stats_done], inc="vp")
            r2 = P.op("vector", lambda e: e.tensor_scalar(
                out=rowsc[0:1, 1024:2048], in0=psq[0:1, :], scalar1=1.0 / D,
                scalar2=EPS, op0=ALU.mult, op1=ALU.add), waits=[stats_done], inc="vp")
            r3 = P.op("vector", lambda e: e.tensor_tensor(
                out=mbc[0:1, 0:1024], in0=rowsc[0:1, 0:1024],
                in1=rowsc[0:1, 0:1024], op=ALU.mult), waits=[r1], inc="vp")
            r4 = P.op("vector", lambda e: e.tensor_tensor(
                out=rowsc[0:1, 1024:2048], in0=rowsc[0:1, 1024:2048],
                in1=mbc[0:1, 0:1024], op=ALU.subtract), waits=[r2, r3], inc="vp")
            r5 = P.op("scalar", lambda e: e.activation(
                out=mbc[0:1, 0:1024], in_=rowsc[0:1, 1024:2048], func=AF.Ln),
                waits=[r4], inc="ap")
            r6 = P.op("scalar", lambda e: e.activation(
                out=rowsc[0:1, 1024:2048], in_=mbc[0:1, 0:1024], func=AF.Exp,
                scale=-0.5), waits=[r5], inc="ap")
            bm = None
            for h in range(2):
                P.op("tensor", lambda e, h=h: e.matmul(
                    pb[:, h * 512:(h + 1) * 512], ones_s[:],
                    rowsc[0:1, h * 512:(h + 1) * 512], start=True, stop=True),
                    waits=[r1, r6, CONST_L])
                bm = P.op("tensor", lambda e, h=h: e.matmul(
                    pb2[:, h * 512:(h + 1) * 512], ones_s[:],
                    rowsc[0:1, 1024 + h * 512:1536 + h * 512], start=True, stop=True),
                    waits=[r6], inc="tp" if h == 1 else None)
            e1 = P.op("scalar", lambda e: e.copy(out=mbc[:], in_=pb[:]),
                      waits=[bm], inc="ap")
            e2 = P.op("scalar", lambda e: e.copy(out=rbc[:], in_=pb2[:]),
                      waits=[bm], inc="ap")
            lst.close()
            out_done = [None] * NDC
            prev = [None, None, None, None]
            for dc in range(NDC):
                sl = dc % 4
                xt = xtile[:, sl * 1024:(sl + 1) * 1024]
                if sb_src is None:
                    ld = P.op("sync", lambda e, d=xt, dc=dc: e.dma_start(
                        out=d, in_=xsrc[dc * 128:(dc + 1) * 128, :]),
                        waits=[prev[sl], e1] + (extra_ld_dep or []),
                        inc=f"s_x{sl}")
                    a1 = P.op("vector", lambda e, s=xt: e.tensor_tensor(
                        out=s, in0=s, in1=mbc[:], op=ALU.add),
                        waits=[ld, e2], inc="vp")
                else:
                    src, w0 = sb_src(dc)
                    a1 = P.op("vector", lambda e, s=xt, src=src: e.tensor_tensor(
                        out=s, in0=src, in1=mbc[:], op=ALU.add),
                        waits=[prev[sl], e1, e2] + w0, inc="vp")
                a2 = P.op("vector", lambda e, s=xt, dc=dc: e.tensor_tensor(
                    out=xlnT[:, dc * TPC:(dc + 1) * TPC], in0=s, in1=rbc[:],
                    op=ALU.mult), waits=[a1], inc="vp")
                prev[sl] = a2
                out_done[dc] = a2
            return out_done

        with ExitStack() as ph:
            pmean = PS(ph, "pmean1", [1, 1024])
            psq = PS(ph, "psq1", [1, 1024])
            prev = [None, None, None, None]
            last = None
            cst_lab = [None] * NDC
            for dc in range(NDC):
                sl = dc % 4
                xt = xtile[:, sl * 1024:(sl + 1) * 1024]
                ob = otile[:, dc * TPC:(dc + 1) * TPC]
                ld = P.op("sync", lambda e, d=xt, dc=dc: e.dma_start(
                    out=d, in_=xT_in[dc * 128:(dc + 1) * 128, :]),
                    waits=[prev[sl]], inc=f"s_x{sl}")
                cst = P.op("scalar", lambda e, s=xt, ob=ob: e.copy(
                    out=ob, in_=s), waits=[ld], inc="ap")
                cst_lab[dc] = cst
                sqr = P.op("scalar", lambda e, ob=ob: e.activation(
                    out=xsqb[:], in_=ob, func=AF.Square),
                    waits=[cst, last], inc="ap")
                for h in range(2):
                    P.op("tensor", lambda e, dc=dc, h=h, ob=ob: e.matmul(
                        pmean[0:1, h * 512:(h + 1) * 512], onescol_b[:],
                        ob[:, h * 512:(h + 1) * 512],
                        start=(dc == 0), stop=(dc == NDC - 1)),
                        waits=[cst, CONST_TINY])
                    last = P.op("tensor", lambda e, dc=dc, h=h: e.matmul(
                        psq[0:1, h * 512:(h + 1) * 512], onescol_b[:],
                        xsqb[:, h * 512:(h + 1) * 512],
                        start=(dc == 0), stop=(dc == NDC - 1)),
                        waits=[sqr], inc="tp" if h == 1 else None)
                prev[sl] = cst
            ln1_done = ln_rows_and_apply(
                ph, "L1", pmean, psq, last, None, None,
                sb_src=lambda dc: (otile[:, dc * TPC:(dc + 1) * TPC],
                                   [cst_lab[dc], cst_lab[12 + dc % 4]]))
        P.barrier()

        with ExitStack() as ph:
            wcol = [SB(ph, "wcolA", [128, NDC * 128], BF16),
                    SB(ph, "wcolB", [128, NDC * 128], BF16)]
            wv_sb = SB(ph, "wv_sb", [128, NDC * 512], BF16)
            kloc = SB(ph, "kloc", [128, NPAIR * TPC], BF16)
            vext = SB(ph, "vext", [128, 8 * 65], BF16)
            ropes = SB(ph, "ropes", [128, TPC], BF16)
            pqk = [PS(ph, "pqkA", [128, TPC]), PS(ph, "pqkB", [128, TPC])]
            pv_ = [PS(ph, "pvA", [128, 512]), PS(ph, "pvB", [128, 512])]

            wv_ld = P.op("gpsimd", lambda e: e.dma_start(
                out=wv_sb[:].rearrange("p (c n) -> p c n", n=512),
                in_=wv_t.rearrange("(c p) n -> p c n", p=128)), inc="g_wv")

            def wcol_load(t, src, col, war):
                return [P.op("gpsimd", lambda e, b=wcol[t % 2], s=src, c=col: e.dma_start(
                    out=b[:], in_=s[c * 128:(c + 1) * 128, :]),
                    waits=[war], inc="g_wc0" if t % 2 == 0 else "g_wc1")]

            rope_prev = [None]

            def rope(base_lbl, tile, ctab, stab):
                sw = None
                for q4 in range(4):
                    dst = ropes[q4 * 32:(q4 + 1) * 32, :]
                    src = tile[(q4 ^ 1) * 32:((q4 ^ 1) + 1) * 32, :]
                    sw = P.op("sync", lambda e, d=dst, s=src: e.dma_start(
                        out=d, in_=s), waits=[base_lbl, rope_prev[0]], inc="s_swp")
                o1 = P.op("vector", lambda e, t=tile, st=stab: e.tensor_tensor(
                    out=ropes[:], in0=ropes[:], in1=st[:], op=ALU.mult),
                    waits=[sw, CONST_G], inc="vp")
                o2 = P.op("vector", lambda e, t=tile, c=ctab: e.tensor_tensor(
                    out=t, in0=t, in1=c[:], op=ALU.mult), waits=[sw], inc="vp")
                done = P.op("vector", lambda e, t=tile: e.tensor_tensor(
                    out=t, in0=t, in1=ropes[:], op=ALU.add), waits=[o1, o2], inc="vp")
                rope_prev[0] = done
                return done

            kv_groups = [[0, 1, 2, 3], [4, 5, 6, 7]]
            ag_k_p = [None] * NPAIR
            kmm = [None] * NPAIR
            kevac = [None] * NPAIR
            krope = [None] * NPAIR
            ag_pend = []
            for t in range(NPAIR):
                kw = wcol_load(t, wk_t, t, war=kmm[t - 2] if t >= 2 else None)
                if t >= 2 and ag_pend:
                    ag_pend.pop(0)()
                pp = pqk[t % 2]
                for dc in range(NDC):
                    for h in range(2):
                        kmm[t] = P.op("tensor", lambda e, b=wcol[t % 2], p=pp, dc=dc, h=h: e.matmul(
                            p[:, h * 512:(h + 1) * 512], b[:, dc * 128:(dc + 1) * 128],
                            xlnT[:, dc * TPC + h * 512: dc * TPC + (h + 1) * 512],
                            start=(dc == 0), stop=(dc == NDC - 1)),
                            waits=kw + [ln1_done[dc] if t == 0 else None,
                                        kevac[t - 2] if t >= 2 else None],
                            inc="tp" if (dc == NDC - 1 and h == 1) else None)
                kt = kloc[:, t * TPC:(t + 1) * TPC]
                kevac[t] = P.op("scalar", lambda e, p=pp, d=kt: e.copy(out=d, in_=p[:]),
                                waits=[kmm[t]], inc="ap")
                krope[t] = rope(kevac[t], kt, ck_s, sk_s)
                kst = P.op("sync", lambda e, t=t, d=kt: e.dma_start(
                    out=kp_ag_in[t].ap(), in_=d),
                    waits=[krope[t]], inc="s_kstg")

                def mk_ag(t=t, kst=kst):
                    def go():
                        ag_k_p[t] = P.op("gpsimd", lambda e, t=t: e.collective_compute(
                            "AllGather", ALU.bypass, replica_groups=kv_groups,
                            ins=[kp_ag_in[t].ap().opt()],
                            outs=[kp_ag_out[t].ap().opt()]),
                            waits=[kst], inc="ccs")
                    return go
                ag_pend.append(mk_ag())
            qw01 = [wcol_load(t_, wq_t, t_, war=kmm[NPAIR - 1]) for t_ in range(2)]
            for go in ag_pend:
                go()
            ag_pend = []

            vstage = [None] * NI
            ones_set = None
            for kv in range(8):
                ones_set = P.op("vector", lambda e, kv=kv: e.memset(
                    vext[:, kv * 65 + 64: kv * 65 + 65], 1.0), inc="vp")
            for tb in range(NI):
                pvb = pv_[tb % 2]
                vmm = None
                for dc in range(NDC):
                    vmm = P.op("tensor", lambda e, pvb=pvb, dc=dc, tb=tb: e.matmul(
                        pvb[:, :], xlnT[:, dc * TPC + tb * 128: dc * TPC + (tb + 1) * 128],
                        wv_sb[:, dc * 512:(dc + 1) * 512],
                        start=(dc == 0), stop=(dc == NDC - 1)),
                        waits=[wv_ld, vstage[tb - 2] if tb >= 2 else None],
                        inc="tp" if dc == NDC - 1 else None)
                ev = None
                for kv in range(8):
                    ev = P.op("scalar", lambda e, pvb=pvb, kv=kv: e.copy(
                        out=vext[:, kv * 65: kv * 65 + 64],
                        in_=pvb[:, kv * 64:(kv + 1) * 64]),
                        waits=[vmm, vstage[tb - 1] if (kv == 0 and tb >= 1) else None],
                        inc="ap")
                vstage[tb] = P.op("sync", lambda e, tb=tb: e.dma_start(
                    out=v_ag_in.ap()[tb * 128:(tb + 1) * 128, :], in_=vext[:]),
                    waits=[ev, ones_set], inc="s_vstg")

            ag_v = P.op("gpsimd", lambda e: e.collective_compute(
                "AllGather", ALU.bypass, replica_groups=kv_groups,
                ins=[v_ag_in.ap().opt()], outs=[v_ag_out.ap().opt()]),
                waits=[P.cur("s_vstg")], inc="ccs")

            qmm = [None] * NQB
            qevac = [None] * NQB
            qrope = [None] * NQB
            for t in range(NQB):
                qw = (qw01[t] if t < 2
                      else wcol_load(t, wq_t, t, war=qmm[t - 2]))
                pp = pqk[t % 2]
                for dc in range(NDC):
                    for h in range(2):
                        qmm[t] = P.op("tensor", lambda e, b=wcol[t % 2], p=pp, dc=dc, h=h: e.matmul(
                            p[:, h * 512:(h + 1) * 512], b[:, dc * 128:(dc + 1) * 128],
                            xlnT[:, dc * TPC + h * 512: dc * TPC + (h + 1) * 512],
                            start=(dc == 0), stop=(dc == NDC - 1)),
                            waits=qw + [qevac[t - 2] if t >= 2 else kevac[NPAIR - 1]],
                            inc="tp" if (dc == NDC - 1 and h == 1) else None)
                qt = qT[:, t * TPC:(t + 1) * TPC]
                qevac[t] = P.op("scalar", lambda e, p=pp, d=qt: e.copy(out=d, in_=p[:]),
                                waits=[qmm[t]], inc="ap")
                qrope[t] = rope(qevac[t], qt, cq_s, sq_s)
        P.barrier()
        ropestk.close()

        with ExitStack() as ph:
            kpair = [SB(ph, "kpairA", [128, 4096], BF16),
                     SB(ph, "kpairB", [128, 4096], BF16)]
            vpair = [SB(ph, "vpairA", [128, 32 * 130], BF16),
                     SB(ph, "vpairB", [128, 32 * 130], BF16)]
            ptile = [SB(ph, "ptileA", [128, 2048], BF16),
                     SB(ph, "ptileB", [128, 2048], BF16),
                     SB(ph, "ptileC", [128, 2048], BF16)]
            osb = [SB(ph, "osbA", [64, 1024], BF16), SB(ph, "osbB", [64, 1024], BF16)]
            dnb = [SB(ph, "dnbA", [64, 1024], BF16), SB(ph, "dnbB", [64, 1024], BF16)]
            rdn = SB(ph, "rdn", [64, 2048], BF16)
            pS = [PS(ph, "pSA", [128, 1024]), PS(ph, "pSB", [128, 1024])]
            pOq = [[PS(ph, f"pO{par}{h}", [128, 512]) for h in range(2)]
                   for par in range(2)]
            rb_init = P.op("vector", lambda e: e.memset(dnb[0][:], 1.0), inc="vp")
            rb_init = P.op("vector", lambda e: e.memset(dnb[1][:], 1.0),
                           waits=[rb_init], inc="vp")

            slot = {}
            for p in range(32):
                jj, ii = owner_of(p)
                slot[p] = jj * 8 + ii

            kld, vld = {}, {}
            pair_lastS = [None] * NPAIR
            pair_lastPV = [None] * NPAIR

            def load_pair(pr):
                b = pr % 2
                ks, vs = [], []
                for r in range(4):
                    ks.append(P.op("sync", lambda e, pr=pr, r=r, b=b: e.dma_start(
                        out=kpair[b][:, r * 1024:(r + 1) * 1024],
                        in_=kp_ag_out[pr].ap()[r * 128:(r + 1) * 128, :]),
                        waits=[ag_k_p[pr], pair_lastS[pr - 2] if pr >= 2 else None],
                        inc="s_ak0" if b == 0 else "s_ak1"))
                    vs.append(P.op("sync", lambda e, pr=pr, r=r, b=b: e.dma_start(
                        out=vpair[b][:, r * 1040:(r + 1) * 1040].rearrange(
                            "p (i c) -> p i c", c=130),
                        in_=v_ag_out.ap()[r * TPC:(r + 1) * TPC,
                                          pr * 130:(pr + 1) * 130].rearrange(
                            "(i q) c -> q i c", q=128)),
                        waits=[ag_v, pair_lastPV[pr - 2] if pr >= 2 else None],
                        inc="s_av0" if b == 0 else "s_av1"))
                kld[pr], vld[pr] = ks, vs

            load_pair(0)
            load_pair(1)

            div_done = {}
            ln_done = {}
            re_done = {}
            r2_last = {}
            bank_war = {}
            pend_s = []
            pend_r = []
            pend_v = []
            prev_pv_tail = [None, None, None]
            pS_last = [None, None]
            for pr in range(NPAIR):
                b = pr % 2
                for ihead in range(4):
                    t = pr * 4 + ihead
                    tasks = []
                    for kc in range(NI):
                        segs = ([(128 * kc, 512 - 128 * kc), (512, 512)]
                                if kc < 4 else [(128 * kc, 1024 - 128 * kc)])
                        for (c0, ln) in segs:
                            for m in range(4):
                                tasks.append((kc, c0, ln, m))
                    ntask = len(tasks)
                    exps = [None] * ntask
                    pvdep = [[None] * ntask, [None] * ntask]
                    pvs = [[None] * ntask, [None] * ntask]
                    lastSs = [None] * ntask

                    def emit_Smm(n):
                        kc, c0, ln, m = tasks[n]
                        ps_ = pS[n % 2]
                        s_ = slot[4 * kc + m]
                        lastS = None
                        for par in range(2):
                            rb_ = par * 64
                            w = [pS_last[n % 2]]
                            if n == 0:
                                w += kld[pr]
                            lastS = P.op("tensor", lambda e, s_=s_, c0=c0, ln=ln, b=b, rb_=rb_, t=t, par=par, ps_=ps_: e.matmul(
                                ps_[:, par * 512: par * 512 + ln],
                                kpair[b][rb_:rb_ + 64, s_ * 128:(s_ + 1) * 128],
                                qT[rb_:rb_ + 64, t * TPC + c0: t * TPC + c0 + ln],
                                start=True, stop=True),
                                waits=w, inc="tp" if par == 1 else None)
                        lastSs[n] = lastS

                    def emit_exp(n):
                        kc, c0, ln, m = tasks[n]
                        ps_ = pS[n % 2]
                        pt = ptile[n % 3]
                        ptv = pt[:].rearrange("p (q n) -> p q n", q=2)
                        war = (pvs[1][n - 3] if n >= 3 else prev_pv_tail[n])
                        ex = P.op("scalar", lambda e, ln=ln, pt=ptv, t=t, ps_=ps_: e.activation(
                            out=pt[:, :, 0:ln],
                            in_=ps_[:].rearrange("p (q n) -> p q n", q=2)[:, :, 0:ln],
                            func=AF.Exp),
                            waits=[lastSs[n], war], inc="ap")
                        pS_last[n % 2] = ex
                        exps[n] = ex
                        for par in range(2):
                            dep = ex
                            if c0 == 128 * kc:
                                dep = P.op("vector", lambda e, pt=ptv, kc=kc, m=m, par=par: e.tensor_tensor(
                                    out=pt[:, par, 0:128],
                                    in0=pt[:, par, 0:128],
                                    in1=mask_b[:, (kc * 4 + m) * 128:
                                               (kc * 4 + m + 1) * 128],
                                    op=ALU.mult), waits=[ex], inc="vp")
                            pvdep[par][n] = dep

                    def emit_PV(n):
                        kc, c0, ln, m = tasks[n]
                        pt = ptile[n % 3]
                        ptv = pt[:].rearrange("p (q n) -> p q n", q=2)
                        s_ = slot[4 * kc + m]
                        half = 0 if c0 < 512 else 1
                        c0h = c0 - half * 512
                        st = (kc == 0 and m == 0 and c0h == 0)
                        sp = (half == 0 and kc == 3 and m == 3) or \
                             (half == 1 and kc == 7 and m == 3)
                        for par in range(2):
                            ob = pOq[par][half]
                            w = [pvdep[par][n]]
                            if n == 0:
                                w += vld[pr]
                            if st:
                                w.append(bank_war.get((par, half)))
                            pvs[par][n] = P.op("tensor", lambda e, s_=s_, c0h=c0h, ln=ln, b=b, par=par, ob=ob, st=st, sp=sp, pt=ptv, par_=par: e.matmul(
                                ob[0:65, c0h:c0h + ln],
                                vpair[b][:, s_ * 130 + par * 65:
                                         s_ * 130 + (par + 1) * 65],
                                pt[:, par, 0:ln],
                                start=st, stop=sp),
                                waits=w, inc="tp" if par == 1 else None)
                        pair_lastPV[pr] = pvs[1][n]

                    def emit_evac(par, half, pv_label):
                        ob = pOq[par][half]
                        cs = half * 512
                        ev = P.op("vector", lambda e, ob=ob, par=par, cs=cs: e.tensor_copy(
                            out=osb[par][:, cs:cs + 512], in_=ob[0:64, :]),
                            waits=[pv_label, div_done.get((t - 1, par))], inc="vp")
                        dr = P.op("vector", lambda e, ob=ob, par=par, cs=cs: e.tensor_copy(
                            out=rdn[0:1, par * 1024 + cs: par * 1024 + cs + 512],
                            in_=ob[64:65, :]),
                            waits=[ev, re_done.get(t - 1)], inc="vp")
                        bank_war[(par, half)] = dr
                        return ev, dr

                    def mk_ln(t, drs):
                        def go():
                            ln_done[t] = P.op("scalar", lambda e: e.activation(
                                out=rdn[0:1, :], in_=rdn[0:1, :],
                                func=AF.Ln), waits=drs, inc="ap")
                        return go

                    def mk_re(t):
                        def go():
                            re_done[t] = P.op("scalar", lambda e: e.activation(
                                out=rdn[32:33, :], in_=rdn[0:1, :],
                                func=AF.Exp, scale=-1.0),
                                waits=[ln_done[t], r2_last.get(t - 1)], inc="ap")
                        return go

                    def mk_div(t, evs):
                        def go():
                            re = re_done[t]
                            for par in range(2):
                                d = dnb[par]
                                r2a = P.op("vector", lambda e, par=par, d=d: e.tensor_copy(
                                    out=d[0:1, :],
                                    in_=rdn[32:33, par * 1024:(par + 1) * 1024]),
                                    waits=[re, div_done.get((t - 1, par))], inc="vp")
                                r2b = P.op("vector", lambda e, par=par, d=d: e.tensor_copy(
                                    out=d[32:33, :],
                                    in_=rdn[32:33, par * 1024:(par + 1) * 1024]),
                                    waits=[r2a], inc="vp")
                                r2_last[t] = r2b
                                sh = P.op("vector", lambda e, d=d: e.stream_shuffle(
                                    out=d[:], in_=d[:], mask=[0] * 32),
                                    waits=[r2b, rb_init if t == 0 else None], inc="vp")
                                div_done[(t, par)] = P.op("gpsimd", lambda e, t=t, par=par, d=d: e.tensor_tensor(
                                    out=otile[par * 64:par * 64 + 64, t * TPC:(t + 1) * TPC],
                                    in0=osb[par][:], in1=d[:], op=ALU.mult),
                                    waits=[sh, evs[par * 2], evs[par * 2 + 1],
                                           P.cur("pp")], inc="pp")
                        return go

                    idx_low_end = max(n for n, tk in enumerate(tasks) if tk[1] < 512)
                    evd = {}
                    for n in range(ntask):
                        emit_Smm(n)
                        if n >= 2:
                            emit_PV(n - 2)
                        emit_exp(n)
                        if n == 2:
                            for go in pend_s:
                                go()
                            pend_s = []
                        if n == 8:
                            for go in pend_r:
                                go()
                            pend_r = []
                        if n == 14:
                            for go in pend_v:
                                go()
                            pend_v = []
                        if n == idx_low_end + 2:
                            for par in range(2):
                                evd[(par, 0)] = emit_evac(par, 0, pvs[1][idx_low_end])
                    emit_PV(ntask - 2)
                    emit_PV(ntask - 1)
                    pair_lastS[pr] = exps[ntask - 1]
                    prev_pv_tail = [pvs[1][ntask - 3], pvs[1][ntask - 2],
                                    pvs[1][ntask - 1]]
                    for par in range(2):
                        evd[(par, 1)] = emit_evac(par, 1, pvs[1][ntask - 1])
                    pend_s.append(mk_ln(t, [evd[(p_, h_)][1] for p_ in range(2)
                                            for h_ in range(2)]))
                    pend_r.append(mk_re(t))
                    pend_v.append(mk_div(t, [evd[(0, 0)][0], evd[(0, 1)][0],
                                             evd[(1, 0)][0], evd[(1, 1)][0]]))
                if pr + 2 < NPAIR:
                    load_pair(pr + 2)
            for go in pend_s:
                go()
            for go in pend_r:
                go()
            for go in pend_v:
                go()
        P.barrier()

        with ExitStack() as ph:
            wocol = [SB(ph, "wocolA", [128, NDC * 128], BF16),
                     SB(ph, "wocolB", [128, NDC * 128], BF16)]
            pmean2 = PS(ph, "pmean2", [1, 1024])
            psq2 = PS(ph, "psq2", [1, 1024])
            wo_scope = ExitStack()
            pwo = [PS(wo_scope, "pwoA", [128, 1024]), PS(wo_scope, "pwoB", [128, 1024])]
            womm = [None] * NDC
            tile_free = [None] * NDC
            spills = [None] * NDC
            last_stat = None
            stats_q = []

            def emit_stats(dmb, cst, sqr):
                nonlocal last_stat
                for h in range(2):
                    P.op("tensor", lambda e, dmb=dmb, h=h: e.matmul(
                        pmean2[0:1, h * 512:(h + 1) * 512], onescol_b[:],
                        xbtile[:, h * 512:(h + 1) * 512],
                        start=(dmb == 0), stop=(dmb == NDC - 1)), waits=[cst])
                    last_stat = P.op("tensor", lambda e, dmb=dmb, h=h: e.matmul(
                        psq2[0:1, h * 512:(h + 1) * 512], onescol_b[:],
                        xsqb[:, h * 512:(h + 1) * 512],
                        start=(dmb == 0), stop=(dmb == NDC - 1)),
                        waits=[sqr], inc="tp" if h == 1 else None)

            for dmb in range(NDC):
                wol = [P.op("gpsimd", lambda e, b=wocol[dmb % 2], c=dmb: e.dma_start(
                    out=b[:], in_=wo_t[c * 128:(c + 1) * 128, :]),
                    waits=[womm[dmb - 2] if dmb >= 2 else None],
                    inc="g_wo0" if dmb % 2 == 0 else "g_wo1")]
                pp = pwo[dmb % 2]
                for hdc in range(NDC):
                    for h in range(2):
                        womm[dmb] = P.op("tensor", lambda e, b=wocol[dmb % 2], p=pp, hdc=hdc, h=h: e.matmul(
                            p[:, h * 512:(h + 1) * 512], b[:, hdc * 128:(hdc + 1) * 128],
                            otile[:, hdc * TPC + h * 512: hdc * TPC + (h + 1) * 512],
                            start=(hdc == 0), stop=(hdc == NDC - 1)),
                            waits=wol + [tile_free[dmb - 2] if dmb >= 2 else None],
                            inc="tp" if (hdc == NDC - 1 and h == 1) else None)
                if stats_q:
                    emit_stats(*stats_q.pop(0))
                sl = dmb % 2
                xt = xtile[:, sl * 1024:(sl + 1) * 1024]
                xl = P.op("sync", lambda e, d=xt, dmb=dmb: e.dma_start(
                    out=d, in_=xT_in[dmb * 128:(dmb + 1) * 128, :]),
                    waits=[tile_free[dmb - 2] if dmb >= 2 else None,
                           spills[dmb - 2] if dmb >= 2 else None],
                    inc="s_x0" if sl == 0 else "s_x1")
                x2 = P.op("vector", lambda e, d=xt, p=pp: e.tensor_tensor(
                    out=d, in0=d, in1=p[:], op=ALU.add), waits=[xl, womm[dmb]], inc="vp")
                spills[dmb] = P.op("sync", lambda e, s=xt, dmb=dmb: e.dma_start(
                    out=x2spill.ap()[dmb * 128:(dmb + 1) * 128, :], in_=s),
                    waits=[x2], inc="s_sp0" if sl == 0 else "s_sp1")
                cst = P.op("vector", lambda e, s=xt: e.tensor_copy(out=xbtile[:], in_=s),
                           waits=[x2, last_stat], inc="vp")
                sqr = P.op("vector", lambda e: e.tensor_tensor(
                    out=xsqb[:], in0=xbtile[:], in1=xbtile[:], op=ALU.mult),
                    waits=[cst], inc="vp")
                stats_q.append((dmb, cst, sqr))
                tile_free[dmb] = sqr
            while stats_q:
                emit_stats(*stats_q.pop(0))
            wo_scope.close()
            ln2_done = ln_rows_and_apply(ph, "L2", pmean2, psq2, last_stat,
                                         x2spill.ap(),
                                         [P.cur("s_sp0"), P.cur("s_sp1")])

            ptr = PS(ph, "ptr", [128, 64])
            lgsb = SB(ph, "lgsb", [128, 64])
            lmm = None
            for tb in range(8):
                for dc in range(NDC):
                    lmm = P.op("tensor", lambda e, tb=tb, dc=dc: e.matmul(
                        ptr[:, tb * 8:(tb + 1) * 8],
                        xlnT[:, dc * TPC + tb * 128: dc * TPC + (tb + 1) * 128],
                        rt_b[:, dc * 8:(dc + 1) * 8],
                        start=(dc == 0), stop=(dc == NDC - 1)),
                        waits=[ln2_done[dc]],
                        inc="tp" if dc == NDC - 1 else None)
            lev = P.op("scalar", lambda e: e.copy(out=lgsb[:], in_=ptr[:]),
                       waits=[lmm], inc="ap")
            lgst = P.op("sync", lambda e: e.dma_start(
                out=lg_ag_in.ap().rearrange("(t p) e -> p t e", p=128),
                in_=lgsb[:].rearrange("p (t e) -> p t e", e=8)),
                waits=[lev], inc="s_lg")
            ag_lg = P.op("gpsimd", lambda e: e.collective_compute(
                "AllGather", ALU.bypass, replica_groups=[list(range(8))],
                ins=[lg_ag_in.ap().opt()], outs=[lg_ag_out.ap().opt()]),
                waits=[lgst], inc="ccs")
        P.barrier(skip=("ccs",))
        att.close()

        with ExitStack() as ph:
            r_l = [SB(ph, "r_lA", [128, 8]), SB(ph, "r_lB", [128, 8])]
            r_v = [SB(ph, "r_vA", [128, 8]), SB(ph, "r_vB", [128, 8])]
            r_e = SB(ph, "r_e", [128, 8])
            r_s = SB(ph, "r_s", [128, 24])
            bs_strip = SB(ph, "bs_strip", [1, 1024])
            bs64 = SB(ph, "bs64", [64, 16])
            tot_s = SB(ph, "tot_s", [1, 8])
            offrow = SB(ph, "offrow", [1, 8])
            wtrow = SB(ph, "wtrow", [1, 1024])
            ppos = PS(ph, "ppos", [128, 8])
            pbs1 = PS(ph, "pbs1", [1, 8])
            poff = PS(ph, "poff", [64, 8])
            pwt = PS(ph, "pwt", [128, 1024])
            pwtT = PS(ph, "pwtT", [8, 128])

            gb_src = {}
            for gb in range(NGB):
                bb, p = gb // 32, gb % 32
                jj, ii = owner_of(p)
                gb_src[gb] = (4 * bb + jj) * TPC + ii * 128

            lgl = [None, None]
            mmb_prev = [None, None]
            bse_prev = None
            oh_done = [None] * NGB
            for gb in range(NGB):
                s2 = gb % 2
                lgl[s2] = P.op("sync", lambda e, gb=gb, s2=s2: e.dma_start(
                    out=r_l[s2][:], in_=lg_ag_out.ap()[gb_src[gb]:gb_src[gb] + 128, :]),
                    waits=[ag_lg, oh_done[gb - 2] if gb >= 2 else None],
                    inc="s_rt0" if s2 == 0 else "s_rt1")
                mx = P.op("vector", lambda e, s2=s2: e.max(r_v[s2][:], r_l[s2][:]),
                          waits=[lgl[s2]], inc="vp")
                o1 = P.op("vector", lambda e, gb=gb, s2=s2: e.tensor_scalar(
                    out=ohall[:, gb * 16: gb * 16 + 8], in0=r_l[s2][:],
                    scalar1=r_v[s2][:, 0:1], scalar2=None, op0=ALU.is_equal),
                    waits=[mx], inc="vp")
                o2 = P.op("vector", lambda e, gb=gb, s2=s2: e.tensor_scalar(
                    out=ohall[:, gb * 16 + 8: gb * 16 + 16], in0=r_l[s2][:],
                    scalar1=r_v[s2][:, 1:2], scalar2=None, op0=ALU.is_equal),
                    waits=[mx], inc="vp")
                oh_done[gb] = o2
            cnt_mm = None
            for h in range(2):
                cnt_mm = P.op("tensor", lambda e, h=h: e.matmul(
                    pwt[0:1, h * 512:(h + 1) * 512], onescol_s[:],
                    ohall[:, h * 512:(h + 1) * 512], start=True, stop=True),
                    waits=[oh_done[NGB - 1]], inc="tp")
            bse_prev = P.op("scalar", lambda e: e.copy(
                out=bs_strip[:], in_=pwt[0:1, :]), waits=[cnt_mm], inc="ap")
            bsd = P.op("sync", lambda e: e.dma_start(
                out=scr_dram.ap()[4:5, 0:1024], in_=bs_strip[:]),
                waits=[bse_prev], inc="s_strip")
            bsr = P.op("sync", lambda e: e.dma_start(
                out=bs64[:].rearrange("p (k e) -> p k e", e=8),
                in_=scr_dram.ap()[4:5, 0:1024].rearrange(
                    "o (b k e) -> (o b) k e", k=2, e=8)), waits=[bsd], inc="s_strip")
            om1 = P.op("tensor", lambda e: e.matmul(
                poff[:], ut64s_s[:], bs64[:, 0:8], start=True, stop=True),
                waits=[bsr], inc="tp")
            ot1 = P.op("tensor", lambda e: e.matmul(
                ppos[0:1, :], onescol_s[0:64, :], bs64[:, 0:8], start=True, stop=True),
                waits=[bsr, P.cur("ap")], inc="tp")
            ote = P.op("scalar", lambda e: e.copy(out=tot_s[:], in_=ppos[0:1, :]),
                       waits=[ot1], inc="ap")
            oe1 = P.op("scalar", lambda e: e.copy(out=bs64[:, 0:8], in_=poff[:]),
                       waits=[om1, ot1], inc="ap")
            om2a = P.op("tensor", lambda e: e.matmul(
                poff[:], ut64s_s[:], bs64[:, 8:16], start=True, stop=False),
                waits=[oe1], inc="tp")
            om2b = P.op("tensor", lambda e: e.matmul(
                poff[:], ones_s[0:1, 0:64], tot_s[:], start=False, stop=True),
                waits=[ote], inc="tp")
            oe2 = P.op("scalar", lambda e: e.copy(out=bs64[:, 8:16], in_=poff[:]),
                       waits=[om2b], inc="ap")

            wt_done = []
            own_prev = None
            for i in range(NI):
                s2 = 0
                lgo = P.op("sync", lambda e, i=i: e.dma_start(
                    out=r_l[0][:], in_=lg_ag_in.ap()[i * 128:(i + 1) * 128, :]),
                    waits=[ag_lg, own_prev if own_prev else oh_done[NGB - 2]],
                    inc="s_rt0")
                mx = P.op("vector", lambda e: e.max(r_v[0][:], r_l[0][:]),
                          waits=[lgo, oh_done[NGB - 1]], inc="vp")
                ng = P.op("vector", lambda e: e.tensor_scalar(
                    out=r_s[:, 0:1], in0=r_v[0][:, 0:1], scalar1=-1.0, scalar2=None,
                    op0=ALU.mult), waits=[mx], inc="vp")
                ee = P.op("scalar", lambda e: e.activation(
                    out=r_e[:], in_=r_l[0][:], func=AF.Exp, bias=r_s[:, 0:1]),
                    waits=[ng, bse_prev], inc="ap")
                dn = P.op("vector", lambda e: e.tensor_reduce(
                    r_s[:, 1:2], r_e[:], axis=AX.X, op=ALU.add), waits=[ee], inc="vp")
                p1 = P.op("vector", lambda e: e.reciprocal(r_s[:, 2:3], r_s[:, 1:2]),
                          waits=[dn], inc="vp")
                e2_ = P.op("scalar", lambda e: e.activation(
                    out=r_s[:, 3:4], in_=r_v[0][:, 1:2], func=AF.Exp, bias=r_s[:, 0:1]),
                    waits=[p1], inc="ap")
                p2 = P.op("vector", lambda e, i=i: e.tensor_tensor(
                    out=wtp[:, 2 * i + 1:2 * i + 2], in0=r_s[:, 3:4], in1=r_s[:, 2:3],
                    op=ALU.mult), waits=[e2_], inc="vp")
                p1c = P.op("vector", lambda e, i=i: e.tensor_copy(
                    out=wtp[:, 2 * i:2 * i + 1], in_=r_s[:, 2:3]), waits=[p1], inc="vp")
                o1 = P.op("vector", lambda e: e.tensor_scalar(
                    out=r_s[:, 8:16], in0=r_l[0][:], scalar1=r_v[0][:, 0:1],
                    scalar2=None, op0=ALU.is_equal), waits=[mx], inc="vp")
                o2 = P.op("vector", lambda e: e.tensor_scalar(
                    out=r_s[:, 16:24], in0=r_l[0][:], scalar1=r_v[0][:, 1:2],
                    scalar2=None, op0=ALU.is_equal), waits=[mx], inc="vp")
                kcol = [None, None]
                sel_prev = None
                for kk in range(2):
                    osel = P.op("tensor", lambda e, i=i, kk=kk: e.matmul(
                        pbs1[:], ownsel_s[:, i:i + 1], bs64[:, 8 * kk:8 * kk + 8],
                        start=True, stop=True),
                        waits=[oe1 if kk == 0 else oe2, P.cur("ap")], inc="tp")
                    ose = P.op("scalar", lambda e: e.copy(out=offrow[:], in_=pbs1[:]),
                               waits=[osel], inc="ap")
                    m1 = P.op("tensor", lambda e, kk=kk: e.matmul(
                        ppos[:], ut128_s[:], r_s[:, 8 + 8 * kk:16 + 8 * kk],
                        start=True, stop=False), waits=[o1 if kk == 0 else o2,
                                                        sel_prev, ote], inc="tp")
                    m2 = P.op("tensor", lambda e: e.matmul(
                        ppos[:], ones_s[:], offrow[:], start=False, stop=True),
                        waits=[ose], inc="tp")
                    kp = P.op("vector", lambda e: e.tensor_scalar(
                        out=r_e[:], in0=ppos[:], scalar1=float(CAP), scalar2=None,
                        op0=ALU.is_le), waits=[m2], inc="vp")
                    sel = P.op("vector", lambda e, kk=kk: e.tensor_tensor(
                        out=r_e[:], in0=r_e[:], in1=r_s[:, 8 + 8 * kk:16 + 8 * kk],
                        op=ALU.mult), waits=[kp], inc="vp")
                    kcol[kk] = P.op("vector", lambda e, kk=kk: e.tensor_reduce(
                        r_s[:, 4 + kk:5 + kk], r_e[:], axis=AX.X, op=ALU.add),
                        waits=[sel], inc="vp")
                    sel_prev = kcol[kk]
                t1 = P.op("vector", lambda e, i=i: e.tensor_scalar(
                    out=r_s[:, 6:7], in0=r_s[:, 4:5], scalar1=wtp[:, 2 * i:2 * i + 1],
                    scalar2=None, op0=ALU.mult), waits=[kcol[0], p1c], inc="vp")
                wd = P.op("vector", lambda e, i=i: e.scalar_tensor_tensor(
                    out=wt_s[:, i:i + 1], in0=r_s[:, 5:6],
                    scalar=wtp[:, 2 * i + 1:2 * i + 2], in1=r_s[:, 6:7],
                    op0=ALU.mult, op1=ALU.add), waits=[t1, p2, kcol[1]], inc="vp")
                wt_done.append(wd)
                own_prev = wd
            wtt = P.op("tensor", lambda e: e.transpose(
                pwtT[:], wt_s[:], eye_s[:]), waits=wt_done, inc="tp")
            wte = P.op("scalar", lambda e: e.copy(out=mbc[0:8, 0:128], in_=pwtT[:]),
                       waits=[wtt], inc="ap")
            wtd1 = P.op("sync", lambda e: e.dma_start(
                out=scr_dram.ap()[7:8, 0:1024].rearrange("o (p q) -> (o p) q", p=8),
                in_=mbc[0:8, 0:128]), waits=[wte], inc="s_wt")
            wtd2 = P.op("sync", lambda e: e.dma_start(
                out=wtrow[:], in_=scr_dram.ap()[7:8, 0:1024]), waits=[wtd1], inc="s_wt")
            wbm = None
            for h in range(2):
                wbm = P.op("tensor", lambda e, h=h: e.matmul(
                    pwt[:, h * 512:(h + 1) * 512], ones_s[:],
                    wtrow[0:1, h * 512:(h + 1) * 512], start=True, stop=True),
                    waits=[wtd2], inc="tp")
            wtb_done = P.op("scalar", lambda e: e.copy(out=wtbc[:], in_=pwt[:]),
                            waits=[wbm], inc="ap")
        P.barrier()

        with ExitStack() as ph:
            h_sb = SB(ph, "h_sb", [128, 8 * TPC], BF16)
            w1fb = [SB(ph, "w1fbA", [128, NDC * 128], BF16),
                    SB(ph, "w1fbB", [128, NDC * 128], BF16)]
            w2fb = SB(ph, "w2fb", [128, 8 * D], BF16)
            outp = SB(ph, "outp", [128, NDC * TPC])
            ph1 = [PS(ph, "ph1A", [128, TPC]), PS(ph, "ph1B", [128, TPC])]
            po2 = [PS(ph, "po2A", [128, TPC]), PS(ph, "po2B", [128, TPC])]

            h1mm = {}
            gel = {}
            w2l = {}
            o2mm = {}
            oadd = {}
            epi_prev = [None, None, None, None]
            for sc in range(8):
                for fb in range(8):
                    fbg = sc * 8 + fb
                    w1l = P.op("gpsimd", lambda e, b=w1fb[fb % 2], c=fbg: e.dma_start(
                        out=b[:], in_=w1_t[c * 128:(c + 1) * 128, :]),
                        waits=[h1mm.get((sc * 8 + fb - 2))],
                        inc="g_w10" if fb % 2 == 0 else "g_w11")
                    w2l[fbg] = P.op("gpsimd", lambda e, fb=fb, fbg=fbg: e.dma_start(
                        out=w2fb[:, fb * D:(fb + 1) * D],
                        in_=w2_t[fbg * 128:(fbg + 1) * 128, :]),
                        waits=[o2mm.get(sc - 1) if fb == 0 else None], inc="g_w2")
                    pp = ph1[fb % 2]
                    mm = None
                    for dc in range(NDC):
                        for h in range(2):
                            mm = P.op("tensor", lambda e, b=w1fb[fb % 2], p=pp, dc=dc, h=h: e.matmul(
                                p[:, h * 512:(h + 1) * 512], b[:, dc * 128:(dc + 1) * 128],
                                xlnT[:, dc * TPC + h * 512: dc * TPC + (h + 1) * 512],
                                start=(dc == 0), stop=(dc == NDC - 1)),
                                waits=[w1l, gel.get(sc * 8 + fb - 2)],
                                inc="tp" if (dc == NDC - 1 and h == 1) else None)
                    h1mm[fbg] = mm
                    gel[fbg] = P.op("scalar", lambda e, p=pp, fb=fb, fbg=fbg: e.activation(
                        out=h_sb[:, fb * TPC:(fb + 1) * TPC], in_=p[:], func=AF.Gelu,
                        bias=b1_s[:, fbg:fbg + 1]),
                        waits=[mm, o2mm.get(sc - 1) if sc >= 1 else None], inc="ap")
                lastmm = None
                for dmb in range(NDC):
                    pp = po2[dmb % 2]
                    for fb in range(8):
                        for h in range(2):
                            lastmm = P.op("tensor", lambda e, pp=pp, fb=fb, dmb=dmb, h=h: e.matmul(
                                pp[:, h * 512:(h + 1) * 512],
                                w2fb[:, fb * D + dmb * 128: fb * D + (dmb + 1) * 128],
                                h_sb[:, fb * TPC + h * 512: fb * TPC + (h + 1) * 512],
                                start=(fb == 0), stop=(fb == 7)),
                                waits=[gel[sc * 8 + fb], w2l[sc * 8 + 7],
                                       oadd.get((sc, dmb - 2)) if dmb >= 2
                                       else oadd.get((sc - 1, NDC - 2 + dmb))],
                                inc="tp" if (fb == 7 and h == 1) else None)
                    if sc == 0:
                        oadd[(sc, dmb)] = P.op("vector", lambda e, pp=pp, dmb=dmb: e.tensor_scalar(
                            out=outp[:, dmb * TPC:(dmb + 1) * TPC], in0=pp[:],
                            scalar1=b2_s[:, dmb:dmb + 1], scalar2=None, op0=ALU.add),
                            waits=[lastmm], inc="vp")
                    else:
                        oadd[(sc, dmb)] = P.op("vector", lambda e, pp=pp, dmb=dmb: e.tensor_tensor(
                            out=outp[:, dmb * TPC:(dmb + 1) * TPC],
                            in0=outp[:, dmb * TPC:(dmb + 1) * TPC], in1=pp[:],
                            op=ALU.add), waits=[lastmm], inc="vp")
                    if sc == 7:
                        dc = dmb
                        sl = dc % 4
                        xt = xtile[:, sl * 1024:(sl + 1) * 1024]
                        prod = mbc if dc % 2 == 0 else rbc
                        ld = P.op("sync", lambda e, d=xt, dc=dc: e.dma_start(
                            out=d, in_=x2spill.ap()[dc * 128:(dc + 1) * 128, :]),
                            waits=[epi_prev[sl]], inc=f"s_x{sl}")
                        t1 = P.op("vector", lambda e, dc=dc, prod=prod: e.tensor_tensor(
                            out=prod[:], in0=outp[:, dc * TPC:(dc + 1) * TPC],
                            in1=wtbc[:], op=ALU.mult),
                            waits=[oadd[(7, dc)], ld, wtb_done], inc="vp")
                        t2 = P.op("vector", lambda e, s=xt, prod=prod: e.tensor_tensor(
                            out=s, in0=s, in1=prod[:], op=ALU.add),
                            waits=[t1], inc="vp")
                        st = P.op("sync", lambda e, s=xt, dc=dc: e.dma_start(
                            out=xoutT[dc * 128:(dc + 1) * 128, :], in_=s),
                            waits=[t2], inc="s_out0" if sl < 2 else "s_out1")
                        epi_prev[sl] = st
                o2mm[sc] = lastmm
            outs_done = [P.cur("s_out0"), P.cur("s_out1")]
            P.op("sync", None, waits=outs_done)
            P.op("gpsimd", lambda e: e.drain(), waits=outs_done, inc="gp")
            P.op("gpsimd", None, waits=[P.cur("gp")])

        P.emit(block, sems)
    return nc



def _prep_inputs(inputs):
    x = np.asarray(inputs["L0_attn_x"])
    Wq = np.asarray(inputs["L0_attn_Wq"])
    Wk = np.asarray(inputs["L0_attn_Wk"])
    Wv = np.asarray(inputs["L0_attn_Wv"])
    Wo = np.asarray(inputs["L0_attn_Wo"])
    rw = np.asarray(inputs["L0_ffn_router_w"])
    w1 = np.asarray(inputs["L0_ffn_w1"])
    b1 = np.asarray(inputs["L0_ffn_b1"])
    w2 = np.asarray(inputs["L0_ffn_w2"])
    b2 = np.asarray(inputs["L0_ffn_b2"])

    hdperm = np.concatenate([np.arange(0, 64, 2), np.arange(1, 64, 2)])
    qperm = np.zeros(D, np.int64)
    for p in range(4):
        for i2 in range(4):
            t = 4 * p + i2
            gA = 8 * p + i2
            gB = 8 * p + 4 + i2
            qperm[t * 128: t * 128 + 64] = gA * 64 + hdperm
            qperm[t * 128 + 64: t * 128 + 128] = gB * 64 + hdperm
    kperm = np.zeros(512, np.int64)
    for kv in range(8):
        kperm[kv * 64:(kv + 1) * 64] = kv * 64 + hdperm
    operm = np.zeros(D, np.int64)
    for p in range(4):
        for i2 in range(4):
            t = 4 * p + i2
            gA = 8 * p + i2
            gB = 8 * p + 4 + i2
            operm[t * 128: t * 128 + 64] = gA * 64 + np.arange(64)
            operm[t * 128 + 64: t * 128 + 128] = gB * 64 + np.arange(64)

    def tilemaj(a):
        dd, nc_ = a.shape
        nt = nc_ // 128
        out = np.empty((nt * 128, dd), a.dtype)
        for t in range(nt):
            out[t * 128:(t + 1) * 128, :] = (
                a[:, t * 128:(t + 1) * 128].reshape(dd // 128, 128, 128)
                .transpose(1, 0, 2).reshape(128, dd))
        return np.ascontiguousarray(out)

    shared = {
        "wq_t": tilemaj(Wq.T[:, qperm]),
        "wk_t": tilemaj(Wk.T[:, kperm]),
        "wv_t": np.ascontiguousarray(Wv.T),
        "wo_t": tilemaj(Wo.T[operm, :]),
        "rt_t": np.ascontiguousarray(
            rw.T.reshape(NDC, 128, E).transpose(1, 0, 2).reshape(128, NDC * E)),
        "w1_t": tilemaj(w1.T),
        "b1_t": np.ascontiguousarray(b1.reshape(NFB, 128).T),
        "w2_t": np.ascontiguousarray(w2.T),
        "b2_t": np.ascontiguousarray(b2.reshape(NDC, 128).T),
        "ut128": np.ascontiguousarray(
            (np.arange(128)[:, None] <= np.arange(128)[None, :]).astype(np.float32)),
        "ut64s": np.ascontiguousarray(
            (np.arange(64)[:, None] < np.arange(64)[None, :]).astype(np.float32)),
        "eye_f": np.eye(128, dtype=np.float32),
        "ones_r": np.ones((1, 128), np.float32),
        "onescol": np.ones((128, 1), np.float32),
    }

    inv = (1.0 / 10000.0 ** (np.arange(0, HD, 2, dtype=np.float64) / HD))
    in_maps = []
    for c in range(8):
        b, j = divmod(c, 4)
        posb = np.concatenate([np.arange(pos_of(i, j) * 128, pos_of(i, j) * 128 + 128)
                               for i in range(NI)])
        xo = np.ascontiguousarray(x[b, posb, :].T)
        f = posb[:, None].astype(np.float64) * inv[None, :]
        cos = np.cos(f).T.astype(np.float32)
        sin = np.sin(f).T.astype(np.float32)
        c2 = np.concatenate([cos, cos, cos, cos], 0)
        s2 = np.concatenate([-sin, sin, -sin, sin], 0)
        mask = np.zeros((128, NI * 4 * 128), np.float32)
        for i in range(NI):
            mi = (i + j) % 4
            for m in range(4):
                blk = mask[:, (i * 4 + m) * 128:(i * 4 + m) * 128 + 128]
                if m < mi:
                    blk[:] = 1.0
                elif m == mi:
                    blk[:] = (np.arange(128)[:, None] <= np.arange(128)[None, :])
        osel = np.zeros((64, 8), np.float32)
        for i in range(NI):
            gb = b * 32 + pos_of(i, j)
            osel[gb, i] = 1.0
        im = dict(shared)
        im.update({
            "xT_in": xo,
            "cq2": np.ascontiguousarray(c2 * SCALE),
            "sq2": np.ascontiguousarray(s2 * SCALE),
            "ck2": np.ascontiguousarray(c2),
            "sk2": np.ascontiguousarray(s2),
            "mask_in": mask,
            "ownsel": osel,
        })
        in_maps.append(im)
    return in_maps


_NC_CACHE = {}


def kernel(**inputs):
    in_maps = _prep_inputs(inputs)
    if "nc" not in _NC_CACHE:
        _NC_CACHE["nc"] = build_nc()
    nc = _NC_CACHE["nc"]
    res = run_bass_kernel_spmd(nc, in_maps, core_ids=list(range(8)))
    out = np.zeros((B, L, D), np.float32)
    for c in range(8):
        b, j = divmod(c, 4)
        xt = res.results[c]["xoutT"]
        for i in range(NI):
            p = pos_of(i, j)
            out[b, p * 128:(p + 1) * 128, :] = xt[:, i * 128:(i + 1) * 128].T
    return out
